# version 05
# baseline (speedup 1.0000x reference)
import numpy as np
import ml_dtypes
import concourse.bass as bass
import concourse.mybir as mybir
from concourse.bass_utils import run_bass_kernel_spmd

F32 = mybir.dt.float32
BF16 = mybir.dt.bfloat16
AX = mybir.AxisListType.X
ALU = mybir.AluOpType
ACT = mybir.ActivationFunctionType

B, S, D = 4, 2048, 2048
H, G, DH = 16, 4, 128
E, I1, SI = 16, 1408, 2816
IC = D // 128
F1C = I1 // 128
EXT = [16 - 2 * s for s in range(8)]
NKB = [(e * 128 + 511) // 512 for e in EXT]
NGRP, GT = 2, 4
KB = 1024
MBIG = 65536.0


class Bld:
    ENG = ("T", "S", "V", "G", "Q")

    def __init__(self, nc, guards):
        self.nc = nc
        self.guards = guards
        self.sems = {}
        self.ops = {e: [] for e in self.ENG}
        self.cnt = {e: 0 for e in self.ENG}
        self.dcnt = {}
        self.hw = {e: {} for e in self.ENG}
        self.buf = {}
        self.pend = {e: [] for e in self.ENG}
        for e in self.ENG:
            self._sem(e)

    def _sem(self, key):
        if key not in self.sems:
            g = self.nc.semaphore("sem_" + key)
            self.sems[key] = g.__enter__()
            self.guards.append(g)
        return self.sems[key]

    def _rec(self, kind, b, tag):
        st = self.buf.setdefault(b, {"w": None, "r": []})
        if kind == "w":
            st["w"] = tag
            st["r"] = []
        else:
            st["r"].append(tag)

    def alias(self, new, *olds):
        r = []
        for o in (new,) + olds:
            st = self.buf.get(o)
            if st is None:
                continue
            if st["w"] is not None:
                r.append(st["w"])
            r.extend(st["r"])
        self.buf[new] = {"w": None, "r": r}

    def emit(self, e, fn, reads=(), writes=(), sig=True, dma=None):
        waits = {}

        def add(sem, val):
            if val > waits.get(sem, 0):
                waits[sem] = val

        for b in reads:
            st = self.buf.get(b)
            if st and st["w"]:
                add(*st["w"])
        for b in writes:
            st = self.buf.get(b)
            if st:
                if st["w"]:
                    add(*st["w"])
                for t in st["r"]:
                    add(*t)
        wl = []
        for sem, val in waits.items():
            if self.hw[e].get(sem, -1) < val:
                wl.append((sem, val))
                self.hw[e][sem] = val
        if sig:
            if dma:
                key = "d_" + (writes[0] if writes else "anon")
                self._sem(key)
                self.dcnt[key] = self.dcnt.get(key, 0) + 16
                val = self.dcnt[key]
                inc = (self.sems[key], 16)
                tag = (key, val)
            else:
                self.cnt[e] += 1
                val = self.cnt[e]
                inc = (self.sems[e], 1)
                tag = (e, val)
            for kind, b in self.pend[e]:
                self._rec(kind, b, tag)
            self.pend[e] = []
            for b in writes:
                self._rec("w", b, tag)
            for b in reads:
                self._rec("r", b, tag)
        else:
            inc = None
            for b in writes:
                self.pend[e].append(("w", b))
            for b in reads:
                self.pend[e].append(("r", b))
        sems = self.sems

        def closure(eng, fn=fn, wl=wl, inc=inc):
            for s_, v_ in wl:
                eng.wait_ge(sems[s_], v_)
            inst = fn(eng)
            if inc is not None:
                inst.then_inc(*inc)

        self.ops[e].append(closure)


def build_nc(debug=(), upto=3):
    nc = bass.Bass("TRN2", target_bir_lowering=False)

    xq_d = nc.dram_tensor("xq", [1024, D], F32, kind="ExternalInput")
    xkv_d = nc.dram_tensor("xkv", [S, D], F32, kind="ExternalInput")
    wqT_d = nc.dram_tensor("wqT", [D, D], F32, kind="ExternalInput")
    wkT_d = nc.dram_tensor("wkT", [D, 512], F32, kind="ExternalInput")
    wvT_d = nc.dram_tensor("wvT", [D, 512], F32, kind="ExternalInput")
    woT_d = nc.dram_tensor("woT", [D, D], F32, kind="ExternalInput")
    gwT_d = nc.dram_tensor("gwT", [D, E], F32, kind="ExternalInput")
    w1T_d = nc.dram_tensor("w1T", [E, D, I1], BF16, kind="ExternalInput")
    w3T_d = nc.dram_tensor("w3T", [E, D, I1], BF16, kind="ExternalInput")
    w2T_d = nc.dram_tensor("w2T", [E, I1, D], BF16, kind="ExternalInput")
    sw1T_d = nc.dram_tensor("sw1T", [D, SI], BF16, kind="ExternalInput")
    sw3T_d = nc.dram_tensor("sw3T", [D, SI], BF16, kind="ExternalInput")
    sw2T_d = nc.dram_tensor("sw2T", [SI, D], BF16, kind="ExternalInput")
    pvec_d = nc.dram_tensor("pvec", [128, 32], F32, kind="ExternalInput")
    ionegf_d = nc.dram_tensor("ionegf", [128, 512], F32, kind="ExternalInput")
    slopearg_d = nc.dram_tensor("slopearg", [128, 16], F32, kind="ExternalInput")
    iota16_d = nc.dram_tensor("iota16", [128, 16], F32, kind="ExternalInput")
    iota16mb_d = nc.dram_tensor("iota16mb", [128, 16], F32, kind="ExternalInput")
    cst_d = nc.dram_tensor("cst", [128, 8], F32, kind="ExternalInput")
    ident_d = nc.dram_tensor("ident", [128, 128], F32, kind="ExternalInput")
    identb_d = nc.dram_tensor("identb", [128, 128], BF16, kind="ExternalInput")
    lt_d = nc.dram_tensor("ltexc", [128, 128], F32, kind="ExternalInput")
    ones_d = nc.dram_tensor("ones128", [128, 128], F32, kind="ExternalInput")
    ebase_d = nc.dram_tensor("ebase16", [128, 16], F32, kind="ExternalInput")
    ebase2_d = nc.dram_tensor("ebase2", [128, 16], F32, kind="ExternalInput")
    tokid_d = nc.dram_tensor("tokid", [128, 8], mybir.dt.int32, kind="ExternalInput")

    out_d = nc.dram_tensor("out", [1024, D], F32, kind="ExternalOutput")
    xres_d = nc.dram_tensor("xres_scratch", [1024, D], F32, kind="Internal")
    sj_d = [nc.dram_tensor(f"sj{j}_scratch", [1024, D], BF16, kind="Internal") for j in range(3)]
    y_d = nc.dram_tensor("y_scratch", [1024, D], F32, kind="Internal")
    arena_d = nc.dram_tensor("arena_scratch", [4096, D], F32, kind="Internal")
    gtab_d = nc.dram_tensor("gtab_scratch", [4096, 1], mybir.dt.int32, kind="Internal")
    dwtab_d = nc.dram_tensor("dwtab_scratch", [4096, 1], F32, kind="Internal")
    dbg = {}
    for name in debug:
        shp = {"kf": [128, 4 * 2048], "v": [128, 16 * 512], "qall": [128, 2048],
               "xres": [1024, D], "s0": [128, 16 * 512], "dw": [128, 128],
               "y": [128, 4 * 2048], "xnf": [128, 16 * 512], "ot": [128, 2048]}[name]
        dt = BF16 if name == "s0" else F32
        dbg[name] = nc.dram_tensor("dbg_" + name, shp, dt, kind="ExternalOutput")

    guards0 = []
    ar_g = nc.sbuf_tensor("arena", [128, 196 * KB], mybir.dt.uint8)
    arena = ar_g.__enter__()
    guards0.append(ar_g)
    ABASE = nc.lookup_mloc(arena).addr
    off = 0

    def T(name, cols, dt=F32):
        nonlocal off
        o = off
        off = (o + cols * mybir.dt.size(dt) + 31) & ~31
        return nc.alloc_sbuf_tensor_at(name, [128, cols], dt, offset=ABASE + o)

    ident = T("c_ident", 128)
    ionegf = T("c_ionegf", 512)
    cst = T("c_cst", 8)
    pvec = T("c_pvec", 32)
    slopearg = T("c_slopearg", 16)
    iota16 = T("c_iota16", 16)
    iota16mb = T("c_iota16mb", 16)
    gw = T("c_gw", IC * 16)
    nslope = T("c_nslope", 16)
    srt = T("c_srt", 16)
    rs128 = T("c_rs128", 8)
    lnst = T("c_lnst", 32)
    dwsb = T("c_dw", 128)
    scsb = T("c_sc", 64)
    identb = T("c_identb", 128, BF16)
    LT = T("c_lt", 128)
    ones = T("c_ones", 128)
    ebase = T("c_ebase", 16)
    tokid = T("c_tokid", 8, mybir.dt.int32)
    m1sb = T("c_m1", 128)
    m2sb = T("c_m2", 128)
    m3sb = T("c_m3", 128)
    ppsb = T("c_pp", 80)
    slotf = T("c_slotf", 24)
    sloti = T("c_sloti", 24, mybir.dt.int32)
    dwk = T("c_dwk", 24)
    tpsf = T("c_tpsf", 24)
    tpsi = T("c_tpsi", 24, mybir.dt.int32)
    ebase2 = T("c_ebase2", 16)
    basesb = T("c_base", 32)
    gtsb = T("c_gt", 32, mybir.dt.int32)
    dtsb = T("c_dt", 32)
    zzsb = T("c_zz", 32, mybir.dt.int32)

    def R(nbytes):
        nonlocal off
        o = off
        off = o + nbytes
        return o

    r0 = R(32 * KB)
    r1 = R(32 * KB)
    r2 = R(32 * KB)
    r3 = R(34 * KB)
    r4 = R(24 * KB)
    r5 = R(17 * KB)
    r6 = R(13 * KB)
    assert off <= 196 * KB, f"sbuf over budget: {off}"

    def at(name, o, cols, dt=F32):
        return nc.alloc_sbuf_tensor_at(name, [128, cols], dt, offset=ABASE + o)

    xkvc = at("xkvc", r0, 4 * 2048)
    xnf = at("xnf", r1, 16 * 512)
    kf = at("kf", r2, 4 * 2048)
    vtok = at("vtok", r3, 16 * 512)
    wkb = [at(f"wkb{i}", r4 + i * 2 * KB, 512) for i in range(2)]
    wvb = [at(f"wvb{i}", r4 + 4 * KB + i * 2 * KB, 512) for i in range(2)]
    xntok = at("xntok", r5, 2048)
    lnsq = at("lnsq", r5 + 8 * KB, 2048)
    xqsl = at("xqsl", r0, 2048)
    qall = at("qall", r0 + 8 * KB, 2048)
    OT = at("OT", r0 + 16 * KB, 2048)
    B0 = at("B0", r0 + 24 * KB, 2048)
    Ssb = [at(f"Ssb{i}", r1 + i * 8 * KB, 2048) for i in range(2)]
    Psb = [at(f"Psb{i}", r1 + 16 * KB + i * 8 * KB, 2048) for i in range(2)]
    wqb = [at(f"wqb{i}", r4 + i * 8 * KB, 2048) for i in range(2)]
    xnqf = at("xnqf", r4 + 16 * KB, 2048)
    PT = at("PT", r6, 2048)
    wofA = at("wofA", r4 + 16 * KB, 2048)
    Dmat2 = at("Dmat2", r6 + 10 * KB + 512, 128)
    wofB = at("wofB", r6, 2048)
    dtile = at("dtile", r6 + 8 * KB, 512)
    Dmat = at("Dmat", r6 + 10 * KB, 128)
    ysb = at("ysb", r0, 4 * 2048)
    s0 = at("s0", r1, 16 * 512, BF16)
    s1 = at("s1", r1 + 16 * KB, 16 * 512, BF16)
    s2 = at("s2", r2, 16 * 512, BF16)
    xrfin = at("xrfin", r2 + 16 * KB, 2048)
    hfs = at("hfs", r2 + 24 * KB, 512)
    has = at("has", r2 + 26 * KB, 512)
    ht1 = at("ht1", r2 + 28 * KB, 512)
    ht2 = at("ht2", r2 + 30 * KB, 512)
    hs = [at(f"hs{j}", r3 + j * 11264, F1C * 512, BF16) for j in range(3)]
    xrt = at("xrt", r4, 2048)
    xn2tok = at("xn2tok", r4 + 8 * KB, 2048)
    xn2f = at("xn2f", r4 + 16 * KB, 2048)
    w13b = [at(f"w13b{i}", r5 + i * 4 * KB, IC * 128, BF16) for i in range(4)]
    w2h = [at(f"w2h{i}", r6 + i * 6 * KB, 6 * 512, BF16) for i in range(2)]
    w2hB = [at(f"w2h{2 + i}", r1 + 16 * KB + i * 6 * KB, 6 * 512, BF16) for i in range(2)]
    sjt = [at(f"s{j}t", r3 + j * 4 * KB, 2048, BF16) for j in range(3)]
    xgp = [at(f"xgp{i}", r1 + i * 4 * KB, 2048, BF16) for i in range(4)]
    xgf = [at(f"xgf{j}", r2 + j * 8 * KB, 16 * 256, BF16) for j in range(3)]
    hse = [at(f"hse{j}", r3 + 16 * KB + j * 5632, F1C * 256, BF16) for j in range(3)]
    arena_sb = at("arena_sb", r4, 2 * 2048)
    yg = at("yg", r2, 2048)
    ygk = at("ygk", r2 + 8 * KB, 2048)

    ps, guards = [], list(guards0)
    for i in range(8):
        g = nc.psum_tensor(f"ps{i}", [128, 512], F32)
        ps.append(g.__enter__())
        guards.append(g)

    b = Bld(nc, guards)
    (SNM, SSQ, SVAR, SSD, SRSTD, SNMR, SNEGMX, SSUM, SRN,
     SM1, SI1, SM2, SSM, SRS) = range(14)

    def st(i):
        return lnst[:, i:i + 1]

    def ap3(h_, o, s1_, c1, s2_, c2, s3_, c3):
        return bass.AP(h_, o, [[s1_, c1], [s2_, c2], [s3_, c3]])

    for dst, src, nm in ((ident, ident_d, "c_ident"), (ionegf, ionegf_d, "c_ionegf"),
                         (cst, cst_d, "c_cst"), (pvec, pvec_d, "c_pvec"),
                         (slopearg, slopearg_d, "c_slopearg"), (iota16, iota16_d, "c_io16"),
                         (iota16mb, iota16mb_d, "c_io16mb"), (identb, identb_d, "c_identb"),
                         (LT, lt_d, "c_lt"), (ones, ones_d, "c_ones"),
                         (ebase, ebase_d, "c_ebase"), (tokid, tokid_d, "c_tokid"),
                         (ebase2, ebase2_d, "c_ebase2")):
        b.emit("G", lambda g, d=dst, s_=src: g.dma_start(d[:], s_.ap()), writes=[nm], dma=True)
    b.emit("G", lambda g: g.dma_start(ap3(gw, 0, IC * 16, 128, 16, IC, 1, 16),
                                      ap3(gwT_d, 0, 16, 128, 128 * 16, IC, 1, 16)),
           writes=["c_gw"], dma=True)
    b.emit("S", lambda s: s.activation(rs128[:, 1:2], cst[:, 1:2], ACT.Sqrt),
           reads=["c_cst"], writes=["c_rs"])
    b.emit("V", lambda v: v.reciprocal(rs128[:, 0:1], rs128[:, 1:2]), reads=["c_rs"], writes=["c_rs"])
    b.emit("S", lambda s: s.activation(srt[:], slopearg[:], ACT.Sqrt),
           reads=["c_slopearg"], writes=["c_srt"])
    b.emit("V", lambda v: v.reciprocal(nslope[:], srt[:]), reads=["c_srt"], writes=["c_nslope"])
    b.emit("V", lambda v: v.tensor_scalar_mul(nslope[:], nslope[:], -1.0),
           reads=["c_nslope"], writes=["c_nslope"])

    def layernorm(xt, xname, ob, outname, sqbuf, sqname):
        b.emit("V", lambda v: v.reduce_sum(st(SNM), xt, axis=AX, negate=True),
               reads=[xname], writes=["lnst"])
        b.emit("S", lambda s: s.activation(sqbuf[:], xt, ACT.Square, accum_out=st(SSQ)),
               reads=[xname], writes=[sqname, "lnst2"])
        b.emit("V", lambda v: v.tensor_scalar_mul(st(SNM), st(SNM), 1.0 / 2048.0),
               reads=["lnst"], writes=["lnst"])
        b.emit("V", lambda v: v.tensor_mul(st(SVAR), st(SNM), st(SNM)), reads=["lnst"], writes=["lnst"])
        b.emit("V", lambda v: v.scalar_tensor_tensor(st(SVAR), st(SSQ), 1.0 / 2048.0, st(SVAR),
                                                     op0=ALU.mult, op1=ALU.subtract),
               reads=["lnst2", "lnst"], writes=["lnst"])
        b.emit("S", lambda s: s.activation(st(SSD), st(SVAR), ACT.Sqrt, bias=cst[:, 0:1]),
               reads=["lnst", "c_cst"], writes=["lnst"])
        b.emit("V", lambda v: v.reciprocal(st(SRSTD), st(SSD)), reads=["lnst"], writes=["lnst"])
        b.emit("V", lambda v: v.tensor_mul(st(SNMR), st(SNM), st(SRSTD)), reads=["lnst"], writes=["lnst"])
        b.emit("S", lambda s: s.activation(ob, xt, ACT.Identity, scale=st(SRSTD), bias=st(SNMR)),
               reads=[xname, "lnst"], writes=[outname])

    def transpose16(srcbuf, srcname, dstbuf, dstname, dcol):
        for icg in range(4):
            pb = ps[4 + (icg % 2)]
            pn = f"ps{4 + (icg % 2)}"
            for k in range(4):
                ic = icg * 4 + k
                b.emit("T", lambda t, ic=ic, k=k, pb=pb:
                       t.transpose(pb[:, k * 128:(k + 1) * 128],
                                   srcbuf[:, ic * 128:(ic + 1) * 128], ident[:]),
                       reads=[srcname, "c_ident"], writes=[pn], sig=(k == 3))
            c0 = dcol(icg * 4)
            stp = dcol(icg * 4 + 1) - c0
            b.emit("V", lambda v, pb=pb, c0=c0, stp=stp:
                   v.tensor_copy(ap3(dstbuf, c0, dstbuf.shape[1], 128, stp, 4, 1, 128),
                                 ap3(pb, 0, 512, 128, 128, 4, 1, 128)),
                   reads=[pn], writes=[dstname])

    for c4 in range(4):
        for t4 in range(4):
            b.emit("G", lambda g, c4=c4, t4=t4:
                   g.dma_start(xkvc[:, t4 * 2048:(t4 + 1) * 2048],
                               xkv_d.ap()[c4 * 512 + t4 * 128: c4 * 512 + (t4 + 1) * 128, :]),
                   writes=[f"xkvc{t4}"], dma=True)
        for t4 in range(4):
            xt = xkvc[:, t4 * 2048:(t4 + 1) * 2048]
            layernorm(xt, f"xkvc{t4}", xntok[:], "xntok", lnsq, "lnsq")
            transpose16(xntok, "xntok", xnf, "xnf", lambda ic, t4=t4: ic * 512 + t4 * 128)
        for ic in range(IC):
            p = ic % 2
            b.emit("Q", lambda q, ic=ic, p=p:
                   q.dma_start(wkb[p][:], wkT_d.ap()[ic * 128:(ic + 1) * 128, :]),
                   writes=[f"wkb{p}"], dma=True)
            for dk in range(4):
                b.emit("T", lambda t, ic=ic, dk=dk, p=p:
                       t.matmul(ps[dk][:], wkb[p][:, dk * 128:(dk + 1) * 128],
                                xnf[:, ic * 512:(ic + 1) * 512],
                                start=(ic == 0), stop=(ic == IC - 1)),
                       reads=[f"wkb{p}", "xnf"], writes=[f"ps{dk}"], sig=(dk == 3))
        for dk in range(4):
            b.emit("S", lambda s, dk=dk, c4=c4:
                   s.copy(kf[:, dk * 2048 + c4 * 512: dk * 2048 + (c4 + 1) * 512], ps[dk][:]),
                   reads=[f"ps{dk}"], writes=["kf"])
        for ic in range(IC):
            p = ic % 2
            b.emit("G", lambda g, ic=ic, p=p:
                   g.dma_start(wvb[p][:], wvT_d.ap()[ic * 128:(ic + 1) * 128, :]),
                   writes=[f"wvb{p}"], dma=True)
            for t4 in range(4):
                b.emit("T", lambda t, ic=ic, t4=t4, p=p:
                       t.matmul(ps[t4][:], xnf[:, ic * 512 + t4 * 128: ic * 512 + (t4 + 1) * 128],
                                wvb[p][:], start=(ic == 0), stop=(ic == IC - 1)),
                       reads=[f"wvb{p}", "xnf"], writes=[f"ps{t4}"], sig=(t4 == 3))
        for t4 in range(4):
            kt = c4 * 4 + t4
            b.emit("V", lambda v, t4=t4, kt=kt:
                   v.tensor_copy(vtok[:, kt * 512:(kt + 1) * 512], ps[t4][:]),
                   reads=[f"ps{t4}"], writes=["vtok"])

    if "kf" in dbg:
        b.emit("G", lambda g: g.dma_start(dbg["kf"].ap(), kf[:]), reads=["kf"], writes=["dbgkf"], dma=True)
    if "v" in dbg:
        b.emit("G", lambda g: g.dma_start(dbg["v"].ap(), vtok[:]), reads=["vtok"], writes=["dbgv"], dma=True)
    if "xnf" in dbg:
        b.emit("G", lambda g: g.dma_start(dbg["xnf"].ap(), xnf[:]), reads=["xnf"], writes=["dbgxnf"], dma=True)

    if upto < 2:
        return _finish(nc, b, guards)
    XKVC = ["xkvc0", "xkvc1", "xkvc2", "xkvc3"]
    b.alias("xqsl", *XKVC)
    b.alias("qall", *XKVC)
    b.alias("OT", *XKVC)
    b.alias("B0", *XKVC)
    for i in range(2):
        b.alias(f"Ssb{i}", "xnf")
        b.alias(f"Psb{i}", "xnf")
    b.alias("wqb0", "wkb0", "wkb1", "wvb0", "wvb1")

    for sl in range(8):
        Ext, nkb = EXT[sl], NKB[sl]
        b.emit("G", lambda g, sl=sl: g.dma_start(xqsl[:], xq_d.ap()[sl * 128:(sl + 1) * 128, :]),
               writes=["xqsl"], dma=True)
        if sl > 0:
            b.alias("xnqf", "wofA")
            b.alias("PT", "wofB")
        layernorm(xqsl[:], "xqsl", xntok[:], "xntok", lnsq, "lnsq")
        transpose16(xntok, "xntok", xnqf, "xnqf", lambda ic: ic * 128)
        for fc in range(16):
            p = fc % 2
            b.emit("Q" if p == 0 else "G", lambda q, fc=fc, p=p:
                   q.dma_start(ap3(wqb[p], 0, 2048, 128, 128, IC, 1, 128),
                               ap3(wqT_d, fc * 128, D, 128, 128 * D, IC, 1, 128)),
                   writes=[f"wqb{p}"], dma=True)
            for ic in range(IC):
                b.emit("T", lambda t, fc=fc, ic=ic, p=p:
                       t.matmul(ps[p][:, 0:128], wqb[p][:, ic * 128:(ic + 1) * 128],
                                xnqf[:, ic * 128:(ic + 1) * 128],
                                start=(ic == 0), stop=(ic == IC - 1)),
                       reads=[f"wqb{p}", "xnqf"], writes=[f"ps{p}"], sig=(ic == IC - 1))
            b.emit("S", lambda s, fc=fc, p=p: s.copy(qall[:, fc * 128:(fc + 1) * 128], ps[p][:, 0:128]),
                   reads=[f"ps{p}"], writes=["qall"])
        if "qall" in dbg and sl == 0:
            b.emit("G", lambda g: g.dma_start(dbg["qall"].ap(), qall[:]), reads=["qall"], writes=["dbgqall"], dma=True)
        for kb in range(nkb):
            W = min(512, Ext * 128 - kb * 512)
            b.emit("S", lambda s, sl=sl, kb=kb, W=W:
                   s.activation(dtile[:, :W], ionegf[:, :W], ACT.Identity,
                                bias=pvec[:, sl * 4 + kb: sl * 4 + kb + 1]),
                   reads=["c_ionegf", "c_pvec"], writes=["dtile"])
            b.emit("S", lambda s, kb=kb, W=W:
                   s.activation(B0[:, kb * 512:kb * 512 + W], dtile[:, :W], ACT.Relu),
                   reads=["dtile"], writes=["B0"])
            b.emit("S", lambda s, kb=kb, W=W:
                   s.activation(B0[:, kb * 512:kb * 512 + W], B0[:, kb * 512:kb * 512 + W], ACT.Sqrt),
                   reads=["B0"], writes=["B0"])
            b.emit("S", lambda s, kb=kb, W=W:
                   s.activation(dtile[:, :W], dtile[:, :W], ACT.Relu, scale=-1e9),
                   reads=["dtile"], writes=["dtile"])
            b.emit("V", lambda v, kb=kb, W=W:
                   v.tensor_add(B0[:, kb * 512:kb * 512 + W], B0[:, kb * 512:kb * 512 + W],
                                dtile[:, :W]),
                   reads=["B0", "dtile"], writes=["B0"])
        def head_scores(h):
            g_ = h // 4
            par = h % 2
            Sb = Ssb[par]
            sn = f"Ssb{par}"
            pscn = f"ps{2 + par}"
            psc = ps[2 + par]
            for kb in range(nkb):
                W = min(512, Ext * 128 - kb * 512)
                b.emit("T", lambda t, h=h, g_=g_, kb=kb, W=W, psc=psc:
                       t.matmul(psc[:, :W], qall[:, h * 128:(h + 1) * 128],
                                kf[:, g_ * 2048 + kb * 512: g_ * 2048 + kb * 512 + W],
                                start=True, stop=True),
                       reads=["qall", "kf"], writes=[pscn])
                b.emit("S", lambda s, kb=kb, W=W, Sb=Sb, psc=psc:
                       s.activation(Sb[:, kb * 512:kb * 512 + W], psc[:, :W], ACT.Identity,
                                    scale=rs128[:, 0:1]),
                       reads=[pscn, "c_rs"], writes=[sn])
                b.emit("V", lambda v, kb=kb, W=W, h=h, Sb=Sb:
                       v.scalar_tensor_tensor(Sb[:, kb * 512:kb * 512 + W],
                                              B0[:, kb * 512:kb * 512 + W],
                                              nslope[:, h:h + 1], Sb[:, kb * 512:kb * 512 + W],
                                              op0=ALU.mult, op1=ALU.add),
                       reads=["B0", "c_nslope", sn], writes=[sn])

        def head_softmax(h):
            par = h % 2
            Sb, Pb = Ssb[par], Psb[par]
            sn, pn = f"Ssb{par}", f"Psb{par}"
            so = 16 + par * 3
            Dm = Dmat if par == 0 else Dmat2
            Dn = "Dmat" if par == 0 else "Dmat2"
            lx = f"lnx{par}"
            b.emit("V", lambda v, Sb=Sb, Ext=Ext, so=so:
                   v.reduce_max(st(so), Sb[:, :Ext * 128], axis=AX, negate=True),
                   reads=[sn], writes=[lx])
            b.emit("S", lambda s, Sb=Sb, Pb=Pb, Ext=Ext, so=so:
                   s.activation(Pb[:, :Ext * 128], Sb[:, :Ext * 128], ACT.Exp,
                                bias=st(so), accum_out=st(so + 1)),
                   reads=[sn, lx], writes=[pn, lx])
            b.emit("V", lambda v, so=so: v.reciprocal(st(so + 2), st(so + 1)), reads=[lx], writes=[lx])
            b.emit("V", lambda v, Dm=Dm, so=so: v.tensor_scalar_mul(Dm[:], ident[:], st(so + 2)),
                   reads=["c_ident", lx], writes=[Dn])

        def head_pav(h):
            g_ = h // 4
            par = h % 2
            Pb = Psb[par]
            pn = f"Psb{par}"
            Dm = Dmat if par == 0 else Dmat2
            Dn = "Dmat" if par == 0 else "Dmat2"
            for ktg in range((Ext + 3) // 4):
                pb = ps[4 + (ktg % 2)]
                ptn = f"ps{4 + (ktg % 2)}"
                n_in = min(4, Ext - ktg * 4)
                for k in range(n_in):
                    kt = ktg * 4 + k
                    b.emit("T", lambda t, kt=kt, k=k, pb=pb, Pb=Pb, Dm=Dm:
                           t.matmul(pb[:, k * 128:(k + 1) * 128],
                                    Pb[:, kt * 128:(kt + 1) * 128], Dm[:],
                                    start=True, stop=True),
                           reads=[pn, Dn], writes=[ptn], sig=(k == n_in - 1))
                b.emit("V", lambda v, ktg=ktg, n_in=n_in, pb=pb:
                       v.tensor_copy(PT[:, ktg * 512:ktg * 512 + n_in * 128], pb[:, :n_in * 128]),
                       reads=[ptn], writes=["PT"])
            pav = ps[6 + par][:, 0:128]
            pavn = f"ps{6 + par}"
            for kt in range(Ext):
                b.emit("T", lambda t, kt=kt, g_=g_, pav=pav, Ext=Ext:
                       t.matmul(pav, vtok[:, kt * 512 + g_ * 128: kt * 512 + (g_ + 1) * 128],
                                PT[:, kt * 128:(kt + 1) * 128],
                                start=(kt == 0), stop=(kt == Ext - 1)),
                       reads=["vtok", "PT"], writes=[pavn], sig=(kt == Ext - 1))
            b.emit("S", lambda s, h=h, pav=pav: s.copy(OT[:, h * 128:(h + 1) * 128], pav),
                   reads=[pavn], writes=["OT"])

        for h in range(16):
            head_scores(h)
            head_softmax(h)
            if h >= 1:
                head_pav(h - 1)
        head_pav(15)
        if "ot" in dbg and sl == 0:
            b.emit("G", lambda g: g.dma_start(dbg["ot"].ap(), OT[:]), reads=["OT"], writes=["dbgot"], dma=True)
        b.alias("wofA", "xnqf")
        b.alias("wofB", "PT")
        for h in range(16):
            p = h % 2
            wof = wofA if p == 0 else wofB
            wofn = "wofA" if p == 0 else "wofB"
            eng_ = "Q" if p == 0 else "G"
            b.emit(eng_, lambda q, h=h, wof=wof:
                   q.dma_start(wof[:], woT_d.ap()[h * 128:(h + 1) * 128, :]),
                   writes=[wofn], dma=True)
            for nf in range(4):
                b.emit("T", lambda t, h=h, nf=nf, wof=wof:
                       t.matmul(ps[nf][:], OT[:, h * 128:(h + 1) * 128],
                                wof[:, nf * 512:(nf + 1) * 512],
                                start=(h == 0), stop=(h == 15)),
                       reads=[wofn, "OT"], writes=[f"ps{nf}"], sig=(nf == 3))
        b.alias("xnqf", "wofA")
        for nf in range(4):
            b.emit("V", lambda v, nf=nf:
                   v.tensor_add(xnqf[:, nf * 512:(nf + 1) * 512], ps[nf][:],
                                xqsl[:, nf * 512:(nf + 1) * 512]),
                   reads=[f"ps{nf}", "xqsl"], writes=["xnqf"])
        b.emit("G", lambda g, sl=sl: g.dma_start(xres_d.ap()[sl * 128:(sl + 1) * 128, :], xnqf[:]),
               reads=["xnqf"], writes=["xres_dram"], dma=True)

    if "xres" in dbg:
        for sl in range(8):
            b.emit("G", lambda g, sl=sl:
                   g.dma_start(dbg["xres"].ap()[sl * 128:(sl + 1) * 128, :],
                               xres_d.ap()[sl * 128:(sl + 1) * 128, :]),
                   reads=["xres_dram"], writes=["dbgxres"], dma=True)

    if upto < 3:
        return _finish(nc, b, guards)
    b.alias("ysb", "xqsl", "qall", "OT", "B0")
    b.alias("s0", "Ssb0", "Ssb1")
    b.alias("s1", "Psb0", "Psb1")
    b.alias("s2", "kf")
    b.alias("xrfin", "kf")
    b.alias("hfs", "kf")
    b.alias("has", "kf")
    b.alias("ht1", "kf")
    b.alias("ht2", "kf")
    for j in range(3):
        b.alias(f"hs{j}", "vtok")
    b.alias("xrt", "wqb0")
    b.alias("xn2tok", "wqb1")
    b.alias("xn2f", "xnqf")
    b.alias("w13b0", "xntok")
    b.alias("w13b1", "xntok")
    b.alias("w2h0", "PT", "dtile", "Dmat", "Dmat2")
    b.alias("w2h1", "PT", "dtile", "Dmat", "Dmat2")

    SJ = (s0, s1, s2)

    def mlp_block(nf1, lhs1_dma, lhs3_dma, w2_dma, scale_fn):
        for f1c in range(nf1):
            p13 = f1c % 2
            b.emit("Q", lambda q, f1c=f1c, p13=p13: lhs1_dma(q, f1c, w13b[p13]),
                   writes=[f"w13b{p13}"], dma=True)
            b.emit("G", lambda g_, f1c=f1c, p13=p13: lhs3_dma(g_, f1c, w13b[2 + p13]),
                   writes=[f"w13b{2 + p13}"], dma=True)
            for wsel, pbank in ((p13, 0), (2 + p13, 1)):
                for j in range(3):
                    for ic in range(IC):
                        b.emit("T", lambda t, wsel=wsel, pbank=pbank, j=j, ic=ic:
                               t.matmul(ps[pbank][:], w13b[wsel][:, ic * 128:(ic + 1) * 128],
                                        SJ[j][:, ic * 512:ic * 512 + 512],
                                        start=(j == 0 and ic == 0),
                                        stop=(j == 2 and ic == IC - 1)),
                               reads=[f"w13b{wsel}", f"s{j}"], writes=[f"ps{pbank}"],
                               sig=(j == 2 and ic == IC - 1))
            b.emit("S", lambda s: s.activation(has[:], ps[0][:], ACT.Silu),
                   reads=["ps0"], writes=["has"])
            b.emit("V", lambda v: v.tensor_mul(hfs[:], has[:], ps[1][:]),
                   reads=["has", "ps1"], writes=["hfs"])
            b.emit("S", lambda s, f1c=f1c: s.copy(hs[0][:, f1c * 512:(f1c + 1) * 512], hfs[:]),
                   reads=["hfs"], writes=["hs0"])
            b.emit("V", lambda v, f1c=f1c:
                   v.tensor_sub(ht1[:], hfs[:], hs[0][:, f1c * 512:(f1c + 1) * 512]),
                   reads=["hfs", "hs0"], writes=["ht1"])
            b.emit("S", lambda s, f1c=f1c: s.copy(hs[1][:, f1c * 512:(f1c + 1) * 512], ht1[:]),
                   reads=["ht1"], writes=["hs1"])
            b.emit("V", lambda v, f1c=f1c:
                   v.tensor_sub(ht2[:], ht1[:], hs[1][:, f1c * 512:(f1c + 1) * 512]),
                   reads=["ht1", "hs1"], writes=["ht2"])
            b.emit("S", lambda s, f1c=f1c: s.copy(hs[2][:, f1c * 512:(f1c + 1) * 512], ht2[:]),
                   reads=["ht2"], writes=["hs2"])
        nh0 = min(6, nf1)
        for nsl in range(4):
            b.emit("Q", lambda q, nsl=nsl: w2_dma(q, nsl, w2h[0], 0, nh0),
                   writes=["w2h0"], dma=True)
            if nf1 > 6:
                b.emit("G", lambda g_, nsl=nsl: w2_dma(g_, nsl, w2h[1], 6, nf1 - 6),
                       writes=["w2h1"], dma=True)
            for Mt in range(4):
                pdn = 2 + ((nsl * 4 + Mt) % 2)
                pd = ps[pdn]
                for f1c in range(nf1):
                    wb_, wbn, fo = (w2h[0], "w2h0", f1c) if f1c < 6 else (w2h[1], "w2h1", f1c - 6)
                    for j in range(3):
                        b.emit("T", lambda t, f1c=f1c, j=j, Mt=Mt, pd=pd, wb_=wb_, fo=fo:
                               t.matmul(pd[:], hs[j][:, f1c * 512 + Mt * 128: f1c * 512 + Mt * 128 + 128],
                                        wb_[:, fo * 512:(fo + 1) * 512],
                                        start=(f1c == 0 and j == 0),
                                        stop=(f1c == nf1 - 1 and j == 2)),
                               reads=[f"hs{j}", wbn], writes=[f"ps{pdn}"],
                               sig=(f1c == nf1 - 1 and j == 2))
                scale_fn(Mt, nsl, pd, f"ps{pdn}")

    TRASH = 60000.0

    for grp in range(NGRP):
        b.alias("lnsq", "w13b2", "w13b3")
        b.alias("s0t", "hs0")
        b.alias("s1t", "hs0", "hs1")
        b.alias("s2t", "hs1", "hs2")
        for tt4 in range(GT):
            tt = grp * GT + tt4
            b.emit("G", lambda g, tt=tt: g.dma_start(xrt[:], xres_d.ap()[tt * 128:(tt + 1) * 128, :]),
                   reads=["xres_dram"], writes=["xrt"], dma=True)
            layernorm(xrt[:], "xrt", xn2tok[:], "xn2tok", lnsq, "lnsq")
            transpose16(xn2tok, "xn2tok", xn2f, "xn2f", lambda ic: ic * 128)
            for ic in range(IC):
                b.emit("T", lambda t, ic=ic, tt4=tt4:
                       t.matmul(ps[7][:, tt4 * 16:(tt4 + 1) * 16], xn2f[:, ic * 128:(ic + 1) * 128],
                                gw[:, ic * 16:(ic + 1) * 16],
                                start=(ic == 0), stop=(ic == IC - 1)),
                       reads=["xn2f", "c_gw"], writes=["ps7"], sig=(ic == IC - 1))
            b.emit("S", lambda s, tt4=tt4:
                   s.copy(ap3(s0, tt4 * 128, s0.shape[1], 128, 512, IC, 1, 128),
                          ap3(xn2f, 0, 2048, 128, 128, IC, 1, 128)),
                   reads=["xn2f"], writes=["s0"])
            b.emit("V", lambda v, tt4=tt4:
                   v.tensor_sub(ap3(lnsq, 0, 2048, 128, 128, IC, 1, 128),
                                ap3(xn2f, 0, 2048, 128, 128, IC, 1, 128),
                                ap3(s0, tt4 * 128, s0.shape[1], 128, 512, IC, 1, 128)),
                   reads=["xn2f", "s0"], writes=["lnsq"])
            b.emit("S", lambda s, tt4=tt4:
                   s.copy(ap3(s1, tt4 * 128, s1.shape[1], 128, 512, IC, 1, 128),
                          ap3(lnsq, 0, 2048, 128, 128, IC, 1, 128)),
                   reads=["lnsq"], writes=["s1"])
            b.emit("V", lambda v, tt4=tt4:
                   v.tensor_sub(ap3(lnsq, 0, 2048, 128, 128, IC, 1, 128),
                                ap3(lnsq, 0, 2048, 128, 128, IC, 1, 128),
                                ap3(s1, tt4 * 128, s1.shape[1], 128, 512, IC, 1, 128)),
                   reads=["lnsq", "s1"], writes=["lnsq"])
            b.emit("S", lambda s, tt4=tt4:
                   s.copy(ap3(s2, tt4 * 128, s2.shape[1], 128, 512, IC, 1, 128),
                          ap3(lnsq, 0, 2048, 128, 128, IC, 1, 128)),
                   reads=["lnsq"], writes=["s2"])
            b.emit("S", lambda s: s.copy(sjt[0][:], xn2tok[:]), reads=["xn2tok"], writes=["s0t"])
            b.emit("V", lambda v: v.tensor_sub(lnsq[:], xn2tok[:], sjt[0][:]),
                   reads=["xn2tok", "s0t"], writes=["lnsq"])
            b.emit("S", lambda s: s.copy(sjt[1][:], lnsq[:]), reads=["lnsq"], writes=["s1t"])
            b.emit("V", lambda v: v.tensor_sub(lnsq[:], lnsq[:], sjt[1][:]),
                   reads=["lnsq", "s1t"], writes=["lnsq"])
            b.emit("S", lambda s: s.copy(sjt[2][:], lnsq[:]), reads=["lnsq"], writes=["s2t"])
            for j in range(3):
                b.emit("G", lambda g, j=j, tt=tt:
                       g.dma_start(sj_d[j].ap()[tt * 128:(tt + 1) * 128, :], sjt[j][:]),
                       reads=[f"s{j}t"], writes=[f"sjd{j}"], dma=True)
            sc = scsb[:, 0:16]
            eq = scsb[:, 16:32]
            cnd = scsb[:, 32:48]
            ta = scsb[:, 48:64]
            sc3 = ppsb[:, 64:80]
            b.emit("S", lambda s, tt4=tt4: s.activation(sc, ps[7][:, tt4 * 16:(tt4 + 1) * 16], ACT.Sigmoid),
                   reads=["ps7"], writes=["sc"])
            b.emit("V", lambda v: v.reduce_max(st(SM1), sc, axis=AX), reads=["sc"], writes=["lnst"])
            b.emit("V", lambda v: v.tensor_scalar(eq, sc, st(SM1), None, op0=ALU.is_equal),
                   reads=["sc", "lnst"], writes=["eq"])
            b.emit("V", lambda v: v.tensor_mul(cnd, eq, iota16mb[:]), reads=["eq", "c_io16mb"], writes=["cnd"])
            b.emit("V", lambda v: v.tensor_scalar_add(cnd, cnd, MBIG), reads=["cnd"], writes=["cnd"])
            b.emit("V", lambda v: v.tensor_reduce(st(SI1), cnd, axis=AX, op=ALU.min),
                   reads=["cnd"], writes=["lnst"])
            b.emit("V", lambda v: v.tensor_scalar(eq, iota16[:], st(SI1), None, op0=ALU.is_equal),
                   reads=["c_io16", "lnst"], writes=["eq"])
            b.emit("V", lambda v, tt=tt: v.tensor_copy(m1sb[:, tt * 16:(tt + 1) * 16], eq),
                   reads=["eq"], writes=["c_m1"])
            b.emit("V", lambda v: v.tensor_scalar_mul(ta, eq, st(SM1)), reads=["eq", "lnst"], writes=["ta"])
            b.emit("V", lambda v: v.scalar_tensor_tensor(cnd, eq, -1e9, sc, op0=ALU.mult, op1=ALU.add),
                   reads=["eq", "sc"], writes=["cnd"])
            b.emit("V", lambda v: v.reduce_max(st(SM2), cnd, axis=AX), reads=["cnd"], writes=["lnst"])
            b.emit("V", lambda v: v.tensor_scalar(eq, cnd, st(SM2), None, op0=ALU.is_equal),
                   reads=["cnd", "lnst"], writes=["eq"])
            b.emit("V", lambda v: v.tensor_mul(sc3, eq, iota16mb[:]), reads=["eq", "c_io16mb"], writes=["sc3"])
            b.emit("V", lambda v: v.tensor_scalar_add(sc3, sc3, MBIG), reads=["sc3"], writes=["sc3"])
            b.emit("V", lambda v: v.tensor_reduce(st(SI1), sc3, axis=AX, op=ALU.min),
                   reads=["sc3"], writes=["lnst"])
            b.emit("V", lambda v: v.tensor_scalar(eq, iota16[:], st(SI1), None, op0=ALU.is_equal),
                   reads=["c_io16", "lnst"], writes=["eq"])
            b.emit("V", lambda v, tt=tt: v.tensor_copy(m2sb[:, tt * 16:(tt + 1) * 16], eq),
                   reads=["eq"], writes=["c_m2"])
            b.emit("V", lambda v: v.tensor_add(st(SSM), st(SM1), st(SM2)), reads=["lnst"], writes=["lnst"])
            b.emit("V", lambda v: v.reciprocal(st(SRS), st(SSM)), reads=["lnst"], writes=["lnst"])
            b.emit("V", lambda v: v.tensor_scalar_mul(eq, eq, st(SM2)), reads=["eq", "lnst"], writes=["eq"])
            b.emit("V", lambda v: v.tensor_add(ta, ta, eq), reads=["ta", "eq"], writes=["ta"])
            b.emit("V", lambda v, tt=tt: v.tensor_scalar_mul(dwsb[:, tt * 16:(tt + 1) * 16], ta, st(SRS)),
                   reads=["ta", "lnst"], writes=["dw"])
            b.emit("V", lambda v, tt=tt: v.scalar_tensor_tensor(cnd, m2sb[:, tt * 16:(tt + 1) * 16],
                                                                -1e9, cnd, op0=ALU.mult, op1=ALU.add),
                   reads=["c_m2", "cnd"], writes=["cnd"])
            b.emit("V", lambda v: v.reduce_max(st(SM2), cnd, axis=AX), reads=["cnd"], writes=["lnst"])
            b.emit("V", lambda v: v.tensor_scalar(eq, cnd, st(SM2), None, op0=ALU.is_equal),
                   reads=["cnd", "lnst"], writes=["eq"])
            b.emit("V", lambda v: v.tensor_mul(sc3, eq, iota16mb[:]), reads=["eq", "c_io16mb"], writes=["sc3"])
            b.emit("V", lambda v: v.tensor_scalar_add(sc3, sc3, MBIG), reads=["sc3"], writes=["sc3"])
            b.emit("V", lambda v: v.tensor_reduce(st(SI1), sc3, axis=AX, op=ALU.min),
                   reads=["sc3"], writes=["lnst"])
            b.emit("V", lambda v, tt=tt: v.tensor_scalar(m3sb[:, tt * 16:(tt + 1) * 16], iota16[:],
                                                         st(SI1), None, op0=ALU.is_equal),
                   reads=["c_io16", "lnst"], writes=["c_m3"])
        b.alias("w13b2", "lnsq")
        b.alias("w13b3", "lnsq")
        for j in range(3):
            b.alias(f"hs{j}", "s0t", "s1t", "s2t")
        for half in range(2):
            def sh_scale(Mt, nsl, pd, pdn, half=half):
                ysl = ysb[:, Mt * 2048 + nsl * 512: Mt * 2048 + (nsl + 1) * 512]
                if half == 0:
                    b.emit("S", lambda s, pd=pd, ysl=ysl: s.copy(ysl, pd[:]),
                           reads=[pdn], writes=["ysb"])
                else:
                    b.emit("V", lambda v, pd=pd, ysl=ysl: v.tensor_add(ysl, ysl, pd[:]),
                           reads=[pdn, "ysb"], writes=["ysb"])

            mlp_block(
                11,
                lambda q, f1c, dst, half=half: q.dma_start(
                    ap3(dst, 0, IC * 128, 128, 128, IC, 1, 128),
                    ap3(sw1T_d, (half * 11 + f1c) * 128, SI, 128, 128 * SI, IC, 1, 128)),
                lambda s, f1c, dst, half=half: s.dma_start(
                    ap3(dst, 0, IC * 128, 128, 128, IC, 1, 128),
                    ap3(sw3T_d, (half * 11 + f1c) * 128, SI, 128, 128 * SI, IC, 1, 128)),
                lambda q, nsl, dst, fbase, nchunk, half=half: q.dma_start(
                    ap3(dst, 0, 6 * 512, 128, 512, nchunk, 1, 512),
                    ap3(sw2T_d, (half * 11 + fbase) * 128 * D + nsl * 512, D, 128, 128 * D, nchunk, 1, 512)),
                sh_scale)
        for tt4 in range(GT):
            tt = grp * GT + tt4
            b.emit("G", lambda g, tt=tt, tt4=tt4:
                   g.dma_start(y_d.ap()[tt * 128:(tt + 1) * 128, :], ysb[:, tt4 * 2048:(tt4 + 1) * 2048]),
                   reads=["ysb"], writes=["y_dram"], dma=True)

    b.emit("G", lambda g: g.memset(zzsb[:], 0), writes=["c_zz"])
    b.emit("V", lambda v: v.memset(ppsb[:, 0:32], 0.0), writes=["pp0"])
    b.emit("G", lambda g: g.dma_start(ap3(gtab_d, 0, 32, 128, 1, 32, 1, 1), ap3(zzsb, 0, 32, 128, 1, 32, 1, 1)),
           reads=["c_zz"], writes=["gtabd"], dma=True)
    b.emit("G", lambda g: g.dma_start(ap3(dwtab_d, 0, 32, 128, 1, 32, 1, 1), ap3(ppsb, 0, 80, 128, 1, 32, 1, 1)),
           reads=["pp0"], writes=["dwtabd"], dma=True)
    b.emit("V", lambda v: v.memset(basesb[:, 0:32], 0.0), writes=["c_base"])
    for rnk in range(2):
        bcol = basesb[0:1, rnk * 16:rnk * 16 + 16]
        for tt in range(8):
            if rnk == 0:
                b.emit("V", lambda v, tt=tt: v.tensor_add(ppsb[:, 64:80], m1sb[:, tt * 16:(tt + 1) * 16],
                                                          m2sb[:, tt * 16:(tt + 1) * 16]),
                       reads=["c_m1", "c_m2"], writes=["sc3"])
            else:
                b.emit("V", lambda v, tt=tt: v.tensor_copy(ppsb[:, 64:80], m3sb[:, tt * 16:(tt + 1) * 16]),
                       reads=["c_m3"], writes=["sc3"])
            b.emit("T", lambda t: t.matmul(ps[6][:, 0:16], LT[:], ppsb[:, 64:80], start=True, stop=False),
                   reads=["c_lt", "sc3"], writes=["ps6"], sig=False)
            b.emit("T", lambda t, bcol=bcol: t.matmul(ps[6][:, 0:16], ones[0:1, :], bcol,
                                                      start=False, stop=True),
                   reads=["c_ones", "c_base"], writes=["ps6"])
            b.emit("T", lambda t: t.matmul(ps[5][0:1, 0:16], ones[:, 0:1], ppsb[:, 64:80],
                                           start=True, stop=True),
                   reads=["c_ones", "sc3"], writes=["ps5"])
            b.emit("V", lambda v: v.tensor_copy(ppsb[:, 0:16], ps[6][:, 0:16]), reads=["ps6"], writes=["pp0"])
            b.emit("V", lambda v, bcol=bcol: v.tensor_add(bcol, bcol, ps[5][0:1, 0:16]),
                   reads=["ps5", "c_base"], writes=["c_base"])
            b.emit("V", lambda v: v.tensor_scalar(ppsb[:, 16:32], ppsb[:, 0:16], 255.5, None, op0=ALU.is_ge),
                   reads=["pp0"], writes=["pp1"])
            b.emit("V", lambda v: v.tensor_add(ppsb[:, 32:48], ppsb[:, 0:16], ebase[:]),
                   reads=["pp0", "c_ebase"], writes=["pp2"])
            b.emit("V", lambda v: v.scalar_tensor_tensor(ppsb[:, 48:64], ppsb[:, 16:32], TRASH,
                                                         ppsb[:, 32:48], op0=ALU.mult, op1=ALU.add),
                   reads=["pp1", "pp2"], writes=["pp3"])
            b.emit("V", lambda v: v.tensor_scalar(ppsb[:, 64:80], ppsb[:, 0:16], 127.5, None, op0=ALU.is_ge),
                   reads=["pp0"], writes=["sc3"])
            b.emit("V", lambda v: v.scalar_tensor_tensor(ppsb[:, 32:48], ppsb[:, 0:16], 32.0,
                                                         ebase2[:], op0=ALU.mult, op1=ALU.add),
                   reads=["pp0", "c_ebase2"], writes=["pp2"])
            b.emit("V", lambda v: v.scalar_tensor_tensor(ppsb[:, 32:48], ppsb[:, 64:80], -4095.0,
                                                         ppsb[:, 32:48], op0=ALU.mult, op1=ALU.add),
                   reads=["sc3", "pp2"], writes=["pp2"])
            b.emit("V", lambda v: v.scalar_tensor_tensor(ppsb[:, 32:48], ppsb[:, 16:32], TRASH,
                                                         ppsb[:, 32:48], op0=ALU.mult, op1=ALU.add),
                   reads=["pp1", "pp2"], writes=["pp2"])
            ranks = ((0, m1sb, "c_m1"), (1, m2sb, "c_m2")) if rnk == 0 else ((2, m3sb, "c_m3"),)
            for k, msb, mnm in ranks:
                b.emit("V", lambda v, tt=tt, msb=msb: v.tensor_mul(ppsb[:, 0:16], msb[:, tt * 16:(tt + 1) * 16],
                                                                   ppsb[:, 48:64]),
                       reads=[mnm, "pp3"], writes=["pp0"])
                b.emit("V", lambda v, tt=tt, k=k: v.reduce_sum(slotf[:, tt * 3 + k: tt * 3 + k + 1],
                                                               ppsb[:, 0:16], axis=AX),
                       reads=["pp0"], writes=["c_slotf"])
                b.emit("V", lambda v, tt=tt, msb=msb: v.tensor_mul(ppsb[:, 0:16], msb[:, tt * 16:(tt + 1) * 16],
                                                                   ppsb[:, 32:48]),
                       reads=[mnm, "pp2"], writes=["pp0"])
                b.emit("V", lambda v, tt=tt, k=k: v.reduce_sum(tpsf[:, tt * 3 + k: tt * 3 + k + 1],
                                                               ppsb[:, 0:16], axis=AX),
                       reads=["pp0"], writes=["c_tpsf"])
                b.emit("V", lambda v, tt=tt, msb=msb: v.tensor_mul(ppsb[:, 0:16], msb[:, tt * 16:(tt + 1) * 16],
                                                                   dwsb[:, tt * 16:(tt + 1) * 16]),
                       reads=[mnm, "dw"], writes=["pp0"])
                b.emit("V", lambda v, tt=tt, k=k: v.reduce_sum(dwk[:, tt * 3 + k: tt * 3 + k + 1],
                                                               ppsb[:, 0:16], axis=AX),
                       reads=["pp0"], writes=["c_dwk"])
        if rnk == 0:
            b.emit("V", lambda v: v.tensor_copy(basesb[:, 16:32], basesb[:, 0:16]),
                   reads=["c_base"], writes=["c_base"])
    b.emit("V", lambda v: v.tensor_copy(sloti[:], slotf[:]), reads=["c_slotf"], writes=["c_sloti"])
    b.emit("V", lambda v: v.tensor_copy(tpsi[:], tpsf[:]), reads=["c_tpsf"], writes=["c_tpsi"])
    for tt in range(8):
        for k in range(3):
            b.emit("G", lambda g, tt=tt, k=k: g.indirect_dma_start(
                gtab_d.ap(), bass.IndirectOffsetOnAxis(tpsi[:, tt * 3 + k: tt * 3 + k + 1], 0),
                tokid[:, tt:tt + 1], None, bounds_check=b.greg, oob_is_err=False),
                   reads=["c_tpsi", "c_tokid"], writes=["gtabd"], dma=True)
            b.emit("G", lambda g, tt=tt, k=k: g.indirect_dma_start(
                dwtab_d.ap(), bass.IndirectOffsetOnAxis(tpsi[:, tt * 3 + k: tt * 3 + k + 1], 0),
                dwk[:, tt * 3 + k: tt * 3 + k + 1], None, bounds_check=b.greg, oob_is_err=False),
                   reads=["c_tpsi", "c_dwk"], writes=["dwtabd"], dma=True)
    b.emit("G", lambda g: g.dma_start(gtsb[:], gtab_d.ap().rearrange("(p c) one -> p (c one)", p=128)),
           reads=["gtabd"], writes=["c_gt"], dma=True)
    b.emit("G", lambda g: g.dma_start(dtsb[:], dwtab_d.ap().rearrange("(p c) one -> p (c one)", p=128)),
           reads=["dwtabd"], writes=["c_dt"], dma=True)

    b.alias("xgp0", "s0", "s1")
    b.alias("xgp1", "s0", "s1")
    b.alias("xgp2", "s0", "s1")
    b.alias("xgp3", "s0", "s1")
    b.alias("xgf0", "s2")
    b.alias("xgf1", "s2")
    b.alias("xgf2", "s2", "xrfin")
    for j in range(3):
        b.alias(f"hse{j}", "hs0", "hs1", "hs2")
    b.alias("arena_sb", "xrt", "xn2tok")
    b.alias("w2h2", "s1")
    b.alias("w2h3", "s1")
    def exp_gather(e):
        for hs_ in range(2):
            for j in range(3):
                par = (e * 6 + hs_ * 3 + j) % 4
                b.emit("G", lambda g, e=e, hs_=hs_, j=j, par=par: g.indirect_dma_start(
                    xgp[par][:], None, sj_d[j].ap(),
                    bass.IndirectOffsetOnAxis(gtsb[:, e * 2 + hs_: e * 2 + hs_ + 1], 0)),
                       reads=["c_gt", f"sjd{j}"], writes=[f"xgp{par}"], dma=True)
                for icg in range(4):
                    pb = ps[4 + (icg % 2)]
                    pn = f"ps{4 + (icg % 2)}"
                    for kk in range(4):
                        ic = icg * 4 + kk
                        b.emit("T", lambda t, ic=ic, kk=kk, pb=pb, par=par:
                               t.matmul(pb[:, kk * 128:(kk + 1) * 128],
                                        xgp[par][:, ic * 128:(ic + 1) * 128], identb[:],
                                        start=True, stop=True),
                               reads=[f"xgp{par}", "c_identb"], writes=[pn], sig=(kk == 3))
                    b.emit("V", lambda v, icg=icg, pb=pb, j=j, hs_=hs_:
                           v.tensor_copy(ap3(xgf[j], icg * 4 * 256 + hs_ * 128, 16 * 256, 128, 256, 4, 1, 128),
                                         ap3(pb, 0, 512, 128, 128, 4, 1, 128)),
                           reads=[pn], writes=[f"xgf{j}"])

    def exp_ab(e):
        for f1c in range(F1C):
            p13 = f1c % 2
            b.emit("Q", lambda q, f1c=f1c, p13=p13, e=e: q.dma_start(
                ap3(w13b[p13], 0, IC * 128, 128, 128, IC, 1, 128),
                ap3(w1T_d, e * D * I1 + f1c * 128, I1, 128, 128 * I1, IC, 1, 128)),
                   writes=[f"w13b{p13}"], dma=True)
            b.emit("G", lambda g_, f1c=f1c, p13=p13, e=e: g_.dma_start(
                ap3(w13b[2 + p13], 0, IC * 128, 128, 128, IC, 1, 128),
                ap3(w3T_d, e * D * I1 + f1c * 128, I1, 128, 128 * I1, IC, 1, 128)),
                   writes=[f"w13b{2 + p13}"], dma=True)
            for wsel, pbank in ((p13, 0), (2 + p13, 1)):
                for j in range(3):
                    for ic in range(IC):
                        b.emit("T", lambda t, wsel=wsel, pbank=pbank, j=j, ic=ic:
                               t.matmul(ps[pbank][:, 0:256], w13b[wsel][:, ic * 128:(ic + 1) * 128],
                                        xgf[j][:, ic * 256:(ic + 1) * 256],
                                        start=(j == 0 and ic == 0),
                                        stop=(j == 2 and ic == IC - 1)),
                               reads=[f"w13b{wsel}", f"xgf{j}"], writes=[f"ps{pbank}"],
                               sig=(j == 2 and ic == IC - 1))
            b.emit("S", lambda s: s.activation(has[:, 0:256], ps[0][:, 0:256], ACT.Silu),
                   reads=["ps0"], writes=["has"])
            b.emit("V", lambda v: v.tensor_mul(hfs[:, 0:256], has[:, 0:256], ps[1][:, 0:256]),
                   reads=["has", "ps1"], writes=["hfs"])
            b.emit("S", lambda s, f1c=f1c: s.copy(hse[0][:, f1c * 256:(f1c + 1) * 256], hfs[:, 0:256]),
                   reads=["hfs"], writes=["hse0"])
            b.emit("V", lambda v, f1c=f1c:
                   v.tensor_sub(ht1[:, 0:256], hfs[:, 0:256], hse[0][:, f1c * 256:(f1c + 1) * 256]),
                   reads=["hfs", "hse0"], writes=["ht1"])
            b.emit("S", lambda s, f1c=f1c: s.copy(hse[1][:, f1c * 256:(f1c + 1) * 256], ht1[:, 0:256]),
                   reads=["ht1"], writes=["hse1"])
            b.emit("V", lambda v, f1c=f1c:
                   v.tensor_sub(ht2[:, 0:256], ht1[:, 0:256], hse[1][:, f1c * 256:(f1c + 1) * 256]),
                   reads=["ht1", "hse1"], writes=["ht2"])
            b.emit("S", lambda s, f1c=f1c: s.copy(hse[2][:, f1c * 256:(f1c + 1) * 256], ht2[:, 0:256]),
                   reads=["ht2"], writes=["hse2"])

    def exp_down(e):
        for nsl in range(4):
            wset = (w2h[0], w2h[1], "w2h0", "w2h1") if (e * 4 + nsl) % 2 == 0 else \
                   (w2hB[0], w2hB[1], "w2h2", "w2h3")
            b.emit("Q", lambda q, nsl=nsl, e=e, wset=wset: q.dma_start(
                ap3(wset[0], 0, 6 * 512, 128, 512, 6, 1, 512),
                ap3(w2T_d, e * I1 * D + nsl * 512, D, 128, 128 * D, 6, 1, 512)),
                   writes=[wset[2]], dma=True)
            b.emit("G", lambda g_, nsl=nsl, e=e, wset=wset: g_.dma_start(
                ap3(wset[1], 0, 6 * 512, 128, 512, 5, 1, 512),
                ap3(w2T_d, e * I1 * D + 6 * 128 * D + nsl * 512, D, 128, 128 * D, 5, 1, 512)),
                   writes=[wset[3]], dma=True)
            for Mt in range(2):
                pdn = 2 + ((nsl * 2 + Mt) % 2)
                pd = ps[pdn]
                for f1c in range(F1C):
                    wb_, wbn, fo = (wset[0], wset[2], f1c) if f1c < 6 else (wset[1], wset[3], f1c - 6)
                    for j in range(3):
                        b.emit("T", lambda t, f1c=f1c, j=j, Mt=Mt, pd=pd, wb_=wb_, fo=fo:
                               t.matmul(pd[:], hse[j][:, f1c * 256 + Mt * 128: f1c * 256 + Mt * 128 + 128],
                                        wb_[:, fo * 512:(fo + 1) * 512],
                                        start=(f1c == 0 and j == 0),
                                        stop=(f1c == F1C - 1 and j == 2)),
                               reads=[f"hse{j}", wbn], writes=[f"ps{pdn}"],
                               sig=(f1c == F1C - 1 and j == 2))
                b.emit("V", lambda v, Mt=Mt, nsl=nsl, pd=pd, e=e:
                       v.tensor_scalar_mul(arena_sb[:, Mt * 2048 + nsl * 512: Mt * 2048 + (nsl + 1) * 512],
                                           pd[:], dtsb[:, e * 2 + Mt: e * 2 + Mt + 1]),
                       reads=[f"ps{pdn}", "c_dt"], writes=["arena_sb"])
        for Mt in range(2):
            b.emit("G", lambda g, e=e, Mt=Mt: g.dma_start(
                arena_d.ap()[e * 256 + Mt * 128: e * 256 + (Mt + 1) * 128, :],
                arena_sb[:, Mt * 2048:(Mt + 1) * 2048]),
                   reads=["arena_sb"], writes=["arenad"], dma=True)

    exp_gather(0)
    for e in range(E):
        exp_ab(e)
        if e + 1 < E:
            exp_gather(e + 1)
        exp_down(e)

    b.alias("yg", "xgf0")
    b.alias("ygk", "xgf1")
    b.alias("xrfin", "xgf2")
    for tt in range(8):
        b.emit("G", lambda g, tt=tt: g.dma_start(yg[:], y_d.ap()[tt * 128:(tt + 1) * 128, :]),
               reads=["y_dram"], writes=["yg"], dma=True)
        for k in range(3):
            b.emit("G", lambda g, tt=tt, k=k: g.indirect_dma_start(
                ygk[:], None, arena_d.ap(),
                bass.IndirectOffsetOnAxis(sloti[:, tt * 3 + k: tt * 3 + k + 1], 0),
                bounds_check=b.greg, oob_is_err=False),
                   reads=["c_sloti", "arenad"], writes=["ygk"], dma=True)
            b.emit("V", lambda v: v.tensor_add(yg[:], yg[:], ygk[:]),
                   reads=["ygk", "yg"], writes=["yg"])
        b.emit("G", lambda g, tt=tt: g.dma_start(xrfin[:], xres_d.ap()[tt * 128:(tt + 1) * 128, :]),
               reads=["xres_dram"], writes=["xrfin"], dma=True)
        b.emit("V", lambda v: v.tensor_add(yg[:], yg[:], xrfin[:]),
               reads=["xrfin", "yg"], writes=["yg"])
        b.emit("G", lambda g, tt=tt: g.dma_start(out_d.ap()[tt * 128:(tt + 1) * 128, :], yg[:]),
               reads=["yg"], writes=["outd"], dma=True)

    return _finish(nc, b, guards)


def _finish(nc, b, guards):
    finals = [(b.sems[k], v) for k, v in sorted(b.dcnt.items())
              if k.startswith("d_") and (k.startswith("d_out") or k.startswith("d_dbg") or k == "d_xres_dram")]
    with nc.Block() as block:
        @block.tensor
        def _(eng):
            for f in b.ops["T"]:
                f(eng)

        @block.scalar
        def _(eng):
            for f in b.ops["S"]:
                f(eng)

        @block.vector
        def _(eng):
            for f in b.ops["V"]:
                f(eng)

        @block.sync
        def _(eng):
            for f in b.ops["Q"]:
                f(eng)

        @block.gpsimd
        def _(eng):
            b.greg = eng.alloc_register("bnd")
            eng.reg_mov(b.greg, 4095)
            for f in b.ops["G"]:
                f(eng)
            for s_, v_ in finals:
                eng.wait_ge(s_, v_)

    for g in reversed(guards):
        g.__exit__(None, None, None)
    return nc


def host_inputs(L0_attn_x, L0_attn_Wq, L0_attn_Wk, L0_attn_Wv, L0_attn_Wo,
                L0_ffn_gate_w, L0_ffn_w1, L0_ffn_w2, L0_ffn_w3,
                L0_ffn_sw1, L0_ffn_sw2, L0_ffn_sw3):
    BF = ml_dtypes.bfloat16
    x = np.ascontiguousarray(np.asarray(L0_attn_x, dtype=np.float32))
    shared = dict(
        wqT=np.ascontiguousarray(np.asarray(L0_attn_Wq, np.float32).T),
        wkT=np.ascontiguousarray(np.asarray(L0_attn_Wk, np.float32).T),
        wvT=np.ascontiguousarray(np.asarray(L0_attn_Wv, np.float32).T),
        woT=np.ascontiguousarray(np.asarray(L0_attn_Wo, np.float32).T),
        gwT=np.ascontiguousarray(np.asarray(L0_ffn_gate_w, np.float32).T),
        w1T=np.ascontiguousarray(np.asarray(L0_ffn_w1, BF).transpose(0, 2, 1)),
        w3T=np.ascontiguousarray(np.asarray(L0_ffn_w3, BF).transpose(0, 2, 1)),
        w2T=np.ascontiguousarray(np.asarray(L0_ffn_w2, BF).transpose(0, 2, 1)),
        sw1T=np.ascontiguousarray(np.asarray(L0_ffn_sw1, BF).T),
        sw3T=np.ascontiguousarray(np.asarray(L0_ffn_sw3, BF).T),
        sw2T=np.ascontiguousarray(np.asarray(L0_ffn_sw2, BF).T),
        ionegf=np.ascontiguousarray(np.tile(-np.arange(512, dtype=np.float32)[None, :], (128, 1))),
        slopearg=np.ascontiguousarray(np.tile((2.0 ** np.arange(1, 17, dtype=np.float32))[None, :], (128, 1))),
        iota16=np.ascontiguousarray(np.tile(np.arange(16, dtype=np.float32)[None, :], (128, 1))),
        iota16mb=np.ascontiguousarray(np.tile(np.arange(16, dtype=np.float32)[None, :] - np.float32(MBIG), (128, 1))),
        cst=np.ascontiguousarray(np.tile(np.array([1e-5, 128.0, 0, 0, 0, 0, 0, 0], np.float32)[None, :], (128, 1))),
        ident=np.eye(128, dtype=np.float32),
        identb=np.eye(128, dtype=BF),
        ltexc=np.ascontiguousarray(np.triu(np.ones((128, 128), np.float32), 1)),
        ones128=np.ones((128, 128), np.float32),
        ebase16=np.ascontiguousarray(np.tile((np.arange(16, dtype=np.float32) * 256.0)[None, :], (128, 1))),
        ebase2=np.ascontiguousarray(np.tile((np.arange(16, dtype=np.float32) * 2.0)[None, :], (128, 1))),
        tokid=np.ascontiguousarray((np.arange(128, dtype=np.int32)[:, None]
                                    + 128 * np.arange(8, dtype=np.int32)[None, :])),
    )
    in_maps, slot_tiles = [], []
    for c in range(8):
        bb, hh = c // 2, c % 2
        gts = [(14 if hh == 0 else 15) - 2 * s for s in range(8)]
        slot_tiles.append(gts)
        xqr = np.concatenate([x[bb, g * 128:(g + 1) * 128, :] for g in gts], axis=0)
        pv = np.zeros((128, 32), np.float32)
        p = np.arange(128, dtype=np.float32)
        for s_ in range(8):
            for kb in range(4):
                pv[:, s_ * 4 + kb] = p + gts[s_] * 128 - 512 * kb
        m = dict(shared)
        m["xq"] = np.ascontiguousarray(xqr)
        m["xkv"] = np.ascontiguousarray(x[bb])
        m["pvec"] = pv
        in_maps.append(m)
    return in_maps, slot_tiles


def kernel(**inputs):
    in_maps, slot_tiles = host_inputs(**inputs)
    nc = build_nc()
    res = run_bass_kernel_spmd(nc, in_maps, core_ids=list(range(8)))
    out = np.empty((B, S, D), dtype=np.float32)
    for c in range(8):
        o = np.asarray(res.results[c]["out"])
        bb = c // 2
        for s_, g in enumerate(slot_tiles[c]):
            out[bb, g * 128:(g + 1) * 128, :] = o[s_ * 128:(s_ + 1) * 128, :]
    return out


# version 06 (final)
# speedup vs baseline: 1.1055x; 1.1055x over previous
import numpy as np
import ml_dtypes
import concourse.bass as bass
import concourse.mybir as mybir
from concourse.bass_utils import run_bass_kernel_spmd

F32 = mybir.dt.float32
BF16 = mybir.dt.bfloat16
AX = mybir.AxisListType.X
ALU = mybir.AluOpType
ACT = mybir.ActivationFunctionType

B, S, D = 4, 2048, 2048
H, G, DH = 16, 4, 128
E, I1, SI = 16, 1408, 2816
IC = D // 128
F1C = I1 // 128
EXT = [16 - 2 * s for s in range(8)]
NKB = [(e * 128 + 511) // 512 for e in EXT]
NGRP, GT = 2, 4
KB = 1024
MBIG = 65536.0


class Bld:
    ENG = ("T", "S", "V", "G", "Q")

    def __init__(self, nc, guards):
        self.nc = nc
        self.guards = guards
        self.sems = {}
        self.ops = {e: [] for e in self.ENG}
        self.cnt = {e: 0 for e in self.ENG}
        self.dcnt = {}
        self.hw = {e: {} for e in self.ENG}
        self.buf = {}
        self.pend = {e: [] for e in self.ENG}
        for e in self.ENG:
            self._sem(e)

    def _sem(self, key):
        if key not in self.sems:
            g = self.nc.semaphore("sem_" + key)
            self.sems[key] = g.__enter__()
            self.guards.append(g)
        return self.sems[key]

    def _rec(self, kind, b, tag):
        st = self.buf.setdefault(b, {"w": None, "r": []})
        if kind == "w":
            st["w"] = tag
            st["r"] = []
        else:
            st["r"].append(tag)

    def alias(self, new, *olds):
        r = []
        for o in (new,) + olds:
            st = self.buf.get(o)
            if st is None:
                continue
            if st["w"] is not None:
                r.append(st["w"])
            r.extend(st["r"])
        self.buf[new] = {"w": None, "r": r}

    def emit(self, e, fn, reads=(), writes=(), sig=True, dma=None):
        waits = {}

        def add(sem, val):
            if val > waits.get(sem, 0):
                waits[sem] = val

        for b in reads:
            st = self.buf.get(b)
            if st and st["w"]:
                add(*st["w"])
        for b in writes:
            st = self.buf.get(b)
            if st:
                if st["w"]:
                    add(*st["w"])
                for t in st["r"]:
                    add(*t)
        wl = []
        for sem, val in waits.items():
            if self.hw[e].get(sem, -1) < val:
                wl.append((sem, val))
                self.hw[e][sem] = val
        if sig:
            if dma:
                key = "d_" + (writes[0] if writes else "anon")
                self._sem(key)
                self.dcnt[key] = self.dcnt.get(key, 0) + 16
                val = self.dcnt[key]
                inc = (self.sems[key], 16)
                tag = (key, val)
            else:
                self.cnt[e] += 1
                val = self.cnt[e]
                inc = (self.sems[e], 1)
                tag = (e, val)
            for kind, b in self.pend[e]:
                self._rec(kind, b, tag)
            self.pend[e] = []
            for b in writes:
                self._rec("w", b, tag)
            for b in reads:
                self._rec("r", b, tag)
        else:
            inc = None
            for b in writes:
                self.pend[e].append(("w", b))
            for b in reads:
                self.pend[e].append(("r", b))
        sems = self.sems

        def closure(eng, fn=fn, wl=wl, inc=inc):
            for s_, v_ in wl:
                eng.wait_ge(sems[s_], v_)
            inst = fn(eng)
            if inc is not None:
                inst.then_inc(*inc)

        self.ops[e].append(closure)


def build_nc(debug=(), upto=3):
    nc = bass.Bass("TRN2", target_bir_lowering=False)

    xq_d = nc.dram_tensor("xq", [1024, D], F32, kind="ExternalInput")
    xkv_d = nc.dram_tensor("xkv", [S, D], F32, kind="ExternalInput")
    wqI_d = nc.dram_tensor("wqI", [16, 128, 2048], F32, kind="ExternalInput")
    wkT_d = nc.dram_tensor("wkT", [D, 512], F32, kind="ExternalInput")
    wvT_d = nc.dram_tensor("wvT", [D, 512], F32, kind="ExternalInput")
    woT_d = nc.dram_tensor("woT", [D, D], F32, kind="ExternalInput")
    gwT_d = nc.dram_tensor("gwT", [D, E], F32, kind="ExternalInput")
    w1I_d = nc.dram_tensor("w1I", [E, F1C, 128, 2048], BF16, kind="ExternalInput")
    w3I_d = nc.dram_tensor("w3I", [E, F1C, 128, 2048], BF16, kind="ExternalInput")
    w2I_d = nc.dram_tensor("w2I", [E, 4, 128, F1C * 512], BF16, kind="ExternalInput")
    sw1I_d = nc.dram_tensor("sw1I", [22, 128, 2048], BF16, kind="ExternalInput")
    sw3I_d = nc.dram_tensor("sw3I", [22, 128, 2048], BF16, kind="ExternalInput")
    sw2I_d = nc.dram_tensor("sw2I", [2, 4, 128, F1C * 512], BF16, kind="ExternalInput")
    pvec_d = nc.dram_tensor("pvec", [128, 32], F32, kind="ExternalInput")
    ionegf_d = nc.dram_tensor("ionegf", [128, 512], F32, kind="ExternalInput")
    slopearg_d = nc.dram_tensor("slopearg", [128, 16], F32, kind="ExternalInput")
    iota16_d = nc.dram_tensor("iota16", [128, 16], F32, kind="ExternalInput")
    iota16mb_d = nc.dram_tensor("iota16mb", [128, 16], F32, kind="ExternalInput")
    cst_d = nc.dram_tensor("cst", [128, 8], F32, kind="ExternalInput")
    ident_d = nc.dram_tensor("ident", [128, 128], F32, kind="ExternalInput")
    identb_d = nc.dram_tensor("identb", [128, 128], BF16, kind="ExternalInput")
    lt_d = nc.dram_tensor("ltexc", [128, 128], F32, kind="ExternalInput")
    ones_d = nc.dram_tensor("ones128", [128, 128], F32, kind="ExternalInput")
    ebase_d = nc.dram_tensor("ebase16", [128, 16], F32, kind="ExternalInput")
    ebase2_d = nc.dram_tensor("ebase2", [128, 16], F32, kind="ExternalInput")
    tokid_d = nc.dram_tensor("tokid", [128, 8], mybir.dt.int32, kind="ExternalInput")

    out_d = nc.dram_tensor("out", [1024, D], F32, kind="ExternalOutput")
    xres_d = nc.dram_tensor("xres_scratch", [1024, D], F32, kind="Internal")
    sj_d = [nc.dram_tensor(f"sj{j}_scratch", [1024, D], BF16, kind="Internal") for j in range(3)]
    y_d = nc.dram_tensor("y_scratch", [1024, D], F32, kind="Internal")
    arena_d = nc.dram_tensor("arena_scratch", [4096, D], F32, kind="Internal")
    gtab_d = nc.dram_tensor("gtab_scratch", [4096, 1], mybir.dt.int32, kind="Internal")
    dwtab_d = nc.dram_tensor("dwtab_scratch", [4096, 1], F32, kind="Internal")
    dbg = {}
    for name in debug:
        shp = {"kf": [128, 4 * 2048], "v": [128, 16 * 512], "qall": [128, 2048],
               "xres": [1024, D], "s0": [128, 16 * 512], "dw": [128, 128],
               "y": [128, 4 * 2048], "xnf": [128, 16 * 512], "ot": [128, 2048]}[name]
        dt = BF16 if name == "s0" else F32
        dbg[name] = nc.dram_tensor("dbg_" + name, shp, dt, kind="ExternalOutput")

    guards0 = []
    ar_g = nc.sbuf_tensor("arena", [128, 196 * KB], mybir.dt.uint8)
    arena = ar_g.__enter__()
    guards0.append(ar_g)
    ABASE = nc.lookup_mloc(arena).addr
    off = 0

    def T(name, cols, dt=F32):
        nonlocal off
        o = off
        off = (o + cols * mybir.dt.size(dt) + 31) & ~31
        return nc.alloc_sbuf_tensor_at(name, [128, cols], dt, offset=ABASE + o)

    ident = T("c_ident", 128)
    ionegf = T("c_ionegf", 512)
    cst = T("c_cst", 8)
    pvec = T("c_pvec", 32)
    slopearg = T("c_slopearg", 16)
    iota16 = T("c_iota16", 16)
    iota16mb = T("c_iota16mb", 16)
    gw = T("c_gw", IC * 16)
    nslope = T("c_nslope", 16)
    srt = T("c_srt", 16)
    rs128 = T("c_rs128", 8)
    lnst = T("c_lnst", 32)
    dwsb = T("c_dw", 128)
    scsb = T("c_sc", 64)
    identb = T("c_identb", 128, BF16)
    LT = T("c_lt", 128)
    ones = T("c_ones", 128)
    ebase = T("c_ebase", 16)
    tokid = T("c_tokid", 8, mybir.dt.int32)
    m1sb = T("c_m1", 128)
    m2sb = T("c_m2", 128)
    m3sb = T("c_m3", 128)
    ppsb = T("c_pp", 80)
    slotf = T("c_slotf", 24)
    sloti = T("c_sloti", 24, mybir.dt.int32)
    dwk = T("c_dwk", 24)
    tpsf = T("c_tpsf", 24)
    tpsi = T("c_tpsi", 24, mybir.dt.int32)
    ebase2 = T("c_ebase2", 16)
    basesb = T("c_base", 32)
    gtsb = T("c_gt", 32, mybir.dt.int32)
    dtsb = T("c_dt", 32)
    zzsb = T("c_zz", 32, mybir.dt.int32)

    def R(nbytes):
        nonlocal off
        o = off
        off = o + nbytes
        return o

    r0 = R(32 * KB)
    r1 = R(32 * KB)
    r2 = R(32 * KB)
    r3 = R(34 * KB)
    r4 = R(24 * KB)
    r5 = R(17 * KB)
    r6 = R(13 * KB)
    assert off <= 196 * KB, f"sbuf over budget: {off}"

    def at(name, o, cols, dt=F32):
        return nc.alloc_sbuf_tensor_at(name, [128, cols], dt, offset=ABASE + o)

    xkvc = at("xkvc", r0, 4 * 2048)
    xnf = at("xnf", r1, 16 * 512)
    kf = at("kf", r2, 4 * 2048)
    vtok = at("vtok", r3, 16 * 512)
    wkb = [at(f"wkb{i}", r4 + i * 2 * KB, 512) for i in range(2)]
    wvb = [at(f"wvb{i}", r4 + 4 * KB + i * 2 * KB, 512) for i in range(2)]
    xntok = at("xntok", r5, 2048)
    lnsq = at("lnsq", r5 + 8 * KB, 2048)
    xqsl = at("xqsl", r0, 2048)
    qall = at("qall", r0 + 8 * KB, 2048)
    OT = at("OT", r0 + 16 * KB, 2048)
    B0 = at("B0", r0 + 24 * KB, 2048)
    Ssb = [at(f"Ssb{i}", r1 + i * 8 * KB, 2048) for i in range(2)]
    Psb = [at(f"Psb{i}", r1 + 16 * KB + i * 8 * KB, 2048) for i in range(2)]
    wqb = [at(f"wqb{i}", r4 + i * 8 * KB, 2048) for i in range(2)]
    xnqf = at("xnqf", r4 + 16 * KB, 2048)
    PT = at("PT", r6, 2048)
    wofA = at("wofA", r4 + 16 * KB, 2048)
    Dmat2 = at("Dmat2", r6 + 10 * KB + 512, 128)
    wofB = at("wofB", r6, 2048)
    dtile = at("dtile", r6 + 8 * KB, 512)
    Dmat = at("Dmat", r6 + 10 * KB, 128)
    ysb = at("ysb", r0, 4 * 2048)
    s0 = at("s0", r1, 16 * 512, BF16)
    s1 = at("s1", r1 + 16 * KB, 16 * 512, BF16)
    s2 = at("s2", r2, 16 * 512, BF16)
    xrfin = at("xrfin", r2 + 16 * KB, 2048)
    hfs = at("hfs", r2 + 24 * KB, 512)
    has = at("has", r2 + 26 * KB, 512)
    ht1 = at("ht1", r2 + 28 * KB, 512)
    ht2 = at("ht2", r2 + 30 * KB, 512)
    hs = [at(f"hs{j}", r3 + j * 11264, F1C * 512, BF16) for j in range(3)]
    xrt = at("xrt", r4, 2048)
    xn2tok = at("xn2tok", r4 + 8 * KB, 2048)
    xn2f = at("xn2f", r4 + 16 * KB, 2048)
    w13b = [at(f"w13b{i}", r5 + i * 4 * KB, IC * 128, BF16) for i in range(4)]
    w2h = [at(f"w2h{i}", r6 + i * 6 * KB, 6 * 512, BF16) for i in range(2)]
    w2hB = [at(f"w2h{2 + i}", r1 + 16 * KB + i * 6 * KB, 6 * 512, BF16) for i in range(2)]
    sjt = [at(f"s{j}t", r3 + j * 4 * KB, 2048, BF16) for j in range(3)]
    xgp = [at(f"xgp{i}", r1 + i * 4 * KB, 2048, BF16) for i in range(4)]
    xgf = [at(f"xgf{j}", r2 + j * 8 * KB, 16 * 256, BF16) for j in range(3)]
    hse = [at(f"hse{j}", r3 + 16 * KB + j * 5632, F1C * 256, BF16) for j in range(3)]
    arena_sb = at("arena_sb", r4, 2 * 2048)
    yg = at("yg", r2, 2048)
    ygk = at("ygk", r2 + 8 * KB, 2048)

    ps, guards = [], list(guards0)
    for i in range(8):
        g = nc.psum_tensor(f"ps{i}", [128, 512], F32)
        ps.append(g.__enter__())
        guards.append(g)

    b = Bld(nc, guards)
    (SNM, SSQ, SVAR, SSD, SRSTD, SNMR, SNEGMX, SSUM, SRN,
     SM1, SI1, SM2, SSM, SRS) = range(14)

    def st(i):
        return lnst[:, i:i + 1]

    def ap3(h_, o, s1_, c1, s2_, c2, s3_, c3):
        return bass.AP(h_, o, [[s1_, c1], [s2_, c2], [s3_, c3]])

    for dst, src, nm in ((ident, ident_d, "c_ident"), (ionegf, ionegf_d, "c_ionegf"),
                         (cst, cst_d, "c_cst"), (pvec, pvec_d, "c_pvec"),
                         (slopearg, slopearg_d, "c_slopearg"), (iota16, iota16_d, "c_io16"),
                         (iota16mb, iota16mb_d, "c_io16mb"), (identb, identb_d, "c_identb"),
                         (LT, lt_d, "c_lt"), (ones, ones_d, "c_ones"),
                         (ebase, ebase_d, "c_ebase"), (tokid, tokid_d, "c_tokid"),
                         (ebase2, ebase2_d, "c_ebase2")):
        b.emit("G", lambda g, d=dst, s_=src: g.dma_start(d[:], s_.ap()), writes=[nm], dma=True)
    b.emit("G", lambda g: g.dma_start(ap3(gw, 0, IC * 16, 128, 16, IC, 1, 16),
                                      ap3(gwT_d, 0, 16, 128, 128 * 16, IC, 1, 16)),
           writes=["c_gw"], dma=True)
    b.emit("S", lambda s: s.activation(rs128[:, 1:2], cst[:, 1:2], ACT.Sqrt),
           reads=["c_cst"], writes=["c_rs"])
    b.emit("V", lambda v: v.reciprocal(rs128[:, 0:1], rs128[:, 1:2]), reads=["c_rs"], writes=["c_rs"])
    b.emit("S", lambda s: s.activation(srt[:], slopearg[:], ACT.Sqrt),
           reads=["c_slopearg"], writes=["c_srt"])
    b.emit("V", lambda v: v.reciprocal(nslope[:], srt[:]), reads=["c_srt"], writes=["c_nslope"])
    b.emit("V", lambda v: v.tensor_scalar_mul(nslope[:], nslope[:], -1.0),
           reads=["c_nslope"], writes=["c_nslope"])

    def layernorm(xt, xname, ob, outname, sqbuf, sqname):
        b.emit("V", lambda v: v.reduce_sum(st(SNM), xt, axis=AX, negate=True),
               reads=[xname], writes=["lnst"])
        b.emit("S", lambda s: s.activation(sqbuf[:], xt, ACT.Square, accum_out=st(SSQ)),
               reads=[xname], writes=[sqname, "lnst2"])
        b.emit("V", lambda v: v.tensor_scalar_mul(st(SNM), st(SNM), 1.0 / 2048.0),
               reads=["lnst"], writes=["lnst"])
        b.emit("V", lambda v: v.tensor_mul(st(SVAR), st(SNM), st(SNM)), reads=["lnst"], writes=["lnst"])
        b.emit("V", lambda v: v.scalar_tensor_tensor(st(SVAR), st(SSQ), 1.0 / 2048.0, st(SVAR),
                                                     op0=ALU.mult, op1=ALU.subtract),
               reads=["lnst2", "lnst"], writes=["lnst"])
        b.emit("S", lambda s: s.activation(st(SSD), st(SVAR), ACT.Sqrt, bias=cst[:, 0:1]),
               reads=["lnst", "c_cst"], writes=["lnst"])
        b.emit("V", lambda v: v.reciprocal(st(SRSTD), st(SSD)), reads=["lnst"], writes=["lnst"])
        b.emit("V", lambda v: v.tensor_mul(st(SNMR), st(SNM), st(SRSTD)), reads=["lnst"], writes=["lnst"])
        b.emit("S", lambda s: s.activation(ob, xt, ACT.Identity, scale=st(SRSTD), bias=st(SNMR)),
               reads=[xname, "lnst"], writes=[outname])

    def transpose16(srcbuf, srcname, dstbuf, dstname, dcol):
        for icg in range(4):
            pb = ps[4 + (icg % 2)]
            pn = f"ps{4 + (icg % 2)}"
            for k in range(4):
                ic = icg * 4 + k
                b.emit("T", lambda t, ic=ic, k=k, pb=pb:
                       t.transpose(pb[:, k * 128:(k + 1) * 128],
                                   srcbuf[:, ic * 128:(ic + 1) * 128], ident[:]),
                       reads=[srcname, "c_ident"], writes=[pn], sig=(k == 3))
            c0 = dcol(icg * 4)
            stp = dcol(icg * 4 + 1) - c0
            b.emit("V", lambda v, pb=pb, c0=c0, stp=stp:
                   v.tensor_copy(ap3(dstbuf, c0, dstbuf.shape[1], 128, stp, 4, 1, 128),
                                 ap3(pb, 0, 512, 128, 128, 4, 1, 128)),
                   reads=[pn], writes=[dstname])

    for c4 in range(4):
        for t4 in range(4):
            b.emit("G", lambda g, c4=c4, t4=t4:
                   g.dma_start(xkvc[:, t4 * 2048:(t4 + 1) * 2048],
                               xkv_d.ap()[c4 * 512 + t4 * 128: c4 * 512 + (t4 + 1) * 128, :]),
                   writes=[f"xkvc{t4}"], dma=True)
        for t4 in range(4):
            xt = xkvc[:, t4 * 2048:(t4 + 1) * 2048]
            layernorm(xt, f"xkvc{t4}", xntok[:], "xntok", lnsq, "lnsq")
            transpose16(xntok, "xntok", xnf, "xnf", lambda ic, t4=t4: ic * 512 + t4 * 128)
        for ic in range(IC):
            p = ic % 2
            b.emit("Q", lambda q, ic=ic, p=p:
                   q.dma_start(wkb[p][:], wkT_d.ap()[ic * 128:(ic + 1) * 128, :]),
                   writes=[f"wkb{p}"], dma=True)
            for dk in range(4):
                b.emit("T", lambda t, ic=ic, dk=dk, p=p:
                       t.matmul(ps[dk][:], wkb[p][:, dk * 128:(dk + 1) * 128],
                                xnf[:, ic * 512:(ic + 1) * 512],
                                start=(ic == 0), stop=(ic == IC - 1)),
                       reads=[f"wkb{p}", "xnf"], writes=[f"ps{dk}"], sig=(dk == 3))
        for dk in range(4):
            b.emit("S", lambda s, dk=dk, c4=c4:
                   s.copy(kf[:, dk * 2048 + c4 * 512: dk * 2048 + (c4 + 1) * 512], ps[dk][:]),
                   reads=[f"ps{dk}"], writes=["kf"])
        for ic in range(IC):
            p = ic % 2
            b.emit("G", lambda g, ic=ic, p=p:
                   g.dma_start(wvb[p][:], wvT_d.ap()[ic * 128:(ic + 1) * 128, :]),
                   writes=[f"wvb{p}"], dma=True)
            for t4 in range(4):
                b.emit("T", lambda t, ic=ic, t4=t4, p=p:
                       t.matmul(ps[t4][:], xnf[:, ic * 512 + t4 * 128: ic * 512 + (t4 + 1) * 128],
                                wvb[p][:], start=(ic == 0), stop=(ic == IC - 1)),
                       reads=[f"wvb{p}", "xnf"], writes=[f"ps{t4}"], sig=(t4 == 3))
        for t4 in range(4):
            kt = c4 * 4 + t4
            b.emit("V", lambda v, t4=t4, kt=kt:
                   v.tensor_copy(vtok[:, kt * 512:(kt + 1) * 512], ps[t4][:]),
                   reads=[f"ps{t4}"], writes=["vtok"])

    if "kf" in dbg:
        b.emit("G", lambda g: g.dma_start(dbg["kf"].ap(), kf[:]), reads=["kf"], writes=["dbgkf"], dma=True)
    if "v" in dbg:
        b.emit("G", lambda g: g.dma_start(dbg["v"].ap(), vtok[:]), reads=["vtok"], writes=["dbgv"], dma=True)
    if "xnf" in dbg:
        b.emit("G", lambda g: g.dma_start(dbg["xnf"].ap(), xnf[:]), reads=["xnf"], writes=["dbgxnf"], dma=True)

    if upto < 2:
        return _finish(nc, b, guards)
    XKVC = ["xkvc0", "xkvc1", "xkvc2", "xkvc3"]
    b.alias("xqsl", *XKVC)
    b.alias("qall", *XKVC)
    b.alias("OT", *XKVC)
    b.alias("B0", *XKVC)
    for i in range(2):
        b.alias(f"Ssb{i}", "xnf")
        b.alias(f"Psb{i}", "xnf")
    b.alias("wqb0", "wkb0", "wkb1", "wvb0", "wvb1")

    for sl in range(8):
        Ext, nkb = EXT[sl], NKB[sl]
        b.emit("G", lambda g, sl=sl: g.dma_start(xqsl[:], xq_d.ap()[sl * 128:(sl + 1) * 128, :]),
               writes=["xqsl"], dma=True)
        if sl > 0:
            b.alias("xnqf", "wofA")
            b.alias("PT", "wofB")
        layernorm(xqsl[:], "xqsl", xntok[:], "xntok", lnsq, "lnsq")
        transpose16(xntok, "xntok", xnqf, "xnqf", lambda ic: ic * 128)
        for fc in range(16):
            p = fc % 2
            b.emit("Q" if p == 0 else "G", lambda q, fc=fc, p=p:
                   q.dma_start(wqb[p][:], bass.AP(wqI_d, fc * 128 * 2048, [[2048, 128], [1, 2048]])),
                   writes=[f"wqb{p}"], dma=True)
            for ic in range(IC):
                b.emit("T", lambda t, fc=fc, ic=ic, p=p:
                       t.matmul(ps[p][:, 0:128], wqb[p][:, ic * 128:(ic + 1) * 128],
                                xnqf[:, ic * 128:(ic + 1) * 128],
                                start=(ic == 0), stop=(ic == IC - 1)),
                       reads=[f"wqb{p}", "xnqf"], writes=[f"ps{p}"], sig=(ic == IC - 1))
            b.emit("S", lambda s, fc=fc, p=p: s.copy(qall[:, fc * 128:(fc + 1) * 128], ps[p][:, 0:128]),
                   reads=[f"ps{p}"], writes=["qall"])
        if "qall" in dbg and sl == 0:
            b.emit("G", lambda g: g.dma_start(dbg["qall"].ap(), qall[:]), reads=["qall"], writes=["dbgqall"], dma=True)
        for kb in range(nkb):
            W = min(512, Ext * 128 - kb * 512)
            b.emit("S", lambda s, sl=sl, kb=kb, W=W:
                   s.activation(dtile[:, :W], ionegf[:, :W], ACT.Identity,
                                bias=pvec[:, sl * 4 + kb: sl * 4 + kb + 1]),
                   reads=["c_ionegf", "c_pvec"], writes=["dtile"])
            b.emit("S", lambda s, kb=kb, W=W:
                   s.activation(B0[:, kb * 512:kb * 512 + W], dtile[:, :W], ACT.Relu),
                   reads=["dtile"], writes=["B0"])
            b.emit("S", lambda s, kb=kb, W=W:
                   s.activation(B0[:, kb * 512:kb * 512 + W], B0[:, kb * 512:kb * 512 + W], ACT.Sqrt),
                   reads=["B0"], writes=["B0"])
            b.emit("S", lambda s, kb=kb, W=W:
                   s.activation(dtile[:, :W], dtile[:, :W], ACT.Relu, scale=-1e9),
                   reads=["dtile"], writes=["dtile"])
            b.emit("V", lambda v, kb=kb, W=W:
                   v.tensor_add(B0[:, kb * 512:kb * 512 + W], B0[:, kb * 512:kb * 512 + W],
                                dtile[:, :W]),
                   reads=["B0", "dtile"], writes=["B0"])
        def head_scores(h):
            g_ = h // 4
            par = h % 2
            Sb = Ssb[par]
            sn = f"Ssb{par}"
            pscn = f"ps{2 + par}"
            psc = ps[2 + par]
            for kb in range(nkb):
                W = min(512, Ext * 128 - kb * 512)
                b.emit("T", lambda t, h=h, g_=g_, kb=kb, W=W, psc=psc:
                       t.matmul(psc[:, :W], qall[:, h * 128:(h + 1) * 128],
                                kf[:, g_ * 2048 + kb * 512: g_ * 2048 + kb * 512 + W],
                                start=True, stop=True),
                       reads=["qall", "kf"], writes=[pscn])
                b.emit("S", lambda s, kb=kb, W=W, Sb=Sb, psc=psc:
                       s.activation(Sb[:, kb * 512:kb * 512 + W], psc[:, :W], ACT.Identity,
                                    scale=rs128[:, 0:1]),
                       reads=[pscn, "c_rs"], writes=[sn])
                b.emit("V", lambda v, kb=kb, W=W, h=h, Sb=Sb:
                       v.scalar_tensor_tensor(Sb[:, kb * 512:kb * 512 + W],
                                              B0[:, kb * 512:kb * 512 + W],
                                              nslope[:, h:h + 1], Sb[:, kb * 512:kb * 512 + W],
                                              op0=ALU.mult, op1=ALU.add),
                       reads=["B0", "c_nslope", sn], writes=[sn])

        def head_softmax(h):
            par = h % 2
            Sb, Pb = Ssb[par], Psb[par]
            sn, pn = f"Ssb{par}", f"Psb{par}"
            so = 16 + par * 3
            Dm = Dmat if par == 0 else Dmat2
            Dn = "Dmat" if par == 0 else "Dmat2"
            lx = f"lnx{par}"
            b.emit("V", lambda v, Sb=Sb, Ext=Ext, so=so:
                   v.reduce_max(st(so), Sb[:, :Ext * 128], axis=AX, negate=True),
                   reads=[sn], writes=[lx])
            b.emit("S", lambda s, Sb=Sb, Pb=Pb, Ext=Ext, so=so:
                   s.activation(Pb[:, :Ext * 128], Sb[:, :Ext * 128], ACT.Exp,
                                bias=st(so), accum_out=st(so + 1)),
                   reads=[sn, lx], writes=[pn, lx])
            b.emit("V", lambda v, so=so: v.reciprocal(st(so + 2), st(so + 1)), reads=[lx], writes=[lx])
            b.emit("V", lambda v, Dm=Dm, so=so: v.tensor_scalar_mul(Dm[:], ident[:], st(so + 2)),
                   reads=["c_ident", lx], writes=[Dn])

        def head_pav(h):
            g_ = h // 4
            par = h % 2
            Pb = Psb[par]
            pn = f"Psb{par}"
            Dm = Dmat if par == 0 else Dmat2
            Dn = "Dmat" if par == 0 else "Dmat2"
            for ktg in range((Ext + 3) // 4):
                pb = ps[4 + (ktg % 2)]
                ptn = f"ps{4 + (ktg % 2)}"
                n_in = min(4, Ext - ktg * 4)
                for k in range(n_in):
                    kt = ktg * 4 + k
                    b.emit("T", lambda t, kt=kt, k=k, pb=pb, Pb=Pb, Dm=Dm:
                           t.matmul(pb[:, k * 128:(k + 1) * 128],
                                    Pb[:, kt * 128:(kt + 1) * 128], Dm[:],
                                    start=True, stop=True),
                           reads=[pn, Dn], writes=[ptn], sig=(k == n_in - 1))
                b.emit("V", lambda v, ktg=ktg, n_in=n_in, pb=pb:
                       v.tensor_copy(PT[:, ktg * 512:ktg * 512 + n_in * 128], pb[:, :n_in * 128]),
                       reads=[ptn], writes=["PT"])
            pav = ps[6 + par][:, 0:128]
            pavn = f"ps{6 + par}"
            for kt in range(Ext):
                b.emit("T", lambda t, kt=kt, g_=g_, pav=pav, Ext=Ext:
                       t.matmul(pav, vtok[:, kt * 512 + g_ * 128: kt * 512 + (g_ + 1) * 128],
                                PT[:, kt * 128:(kt + 1) * 128],
                                start=(kt == 0), stop=(kt == Ext - 1)),
                       reads=["vtok", "PT"], writes=[pavn], sig=(kt == Ext - 1))
            b.emit("S", lambda s, h=h, pav=pav: s.copy(OT[:, h * 128:(h + 1) * 128], pav),
                   reads=[pavn], writes=["OT"])

        for h in range(16):
            head_scores(h)
            head_softmax(h)
            if h >= 1:
                head_pav(h - 1)
        head_pav(15)
        if "ot" in dbg and sl == 0:
            b.emit("G", lambda g: g.dma_start(dbg["ot"].ap(), OT[:]), reads=["OT"], writes=["dbgot"], dma=True)
        b.alias("wofA", "xnqf")
        b.alias("wofB", "PT")
        for h in range(16):
            p = h % 2
            wof = wofA if p == 0 else wofB
            wofn = "wofA" if p == 0 else "wofB"
            eng_ = "Q" if p == 0 else "G"
            b.emit(eng_, lambda q, h=h, wof=wof:
                   q.dma_start(wof[:], woT_d.ap()[h * 128:(h + 1) * 128, :]),
                   writes=[wofn], dma=True)
            for nf in range(4):
                b.emit("T", lambda t, h=h, nf=nf, wof=wof:
                       t.matmul(ps[nf][:], OT[:, h * 128:(h + 1) * 128],
                                wof[:, nf * 512:(nf + 1) * 512],
                                start=(h == 0), stop=(h == 15)),
                       reads=[wofn, "OT"], writes=[f"ps{nf}"], sig=(nf == 3))
        b.alias("xnqf", "wofA")
        for nf in range(4):
            b.emit("V", lambda v, nf=nf:
                   v.tensor_add(xnqf[:, nf * 512:(nf + 1) * 512], ps[nf][:],
                                xqsl[:, nf * 512:(nf + 1) * 512]),
                   reads=[f"ps{nf}", "xqsl"], writes=["xnqf"])
        b.emit("G", lambda g, sl=sl: g.dma_start(xres_d.ap()[sl * 128:(sl + 1) * 128, :], xnqf[:]),
               reads=["xnqf"], writes=["xres_dram"], dma=True)

    if "xres" in dbg:
        for sl in range(8):
            b.emit("G", lambda g, sl=sl:
                   g.dma_start(dbg["xres"].ap()[sl * 128:(sl + 1) * 128, :],
                               xres_d.ap()[sl * 128:(sl + 1) * 128, :]),
                   reads=["xres_dram"], writes=["dbgxres"], dma=True)

    if upto < 3:
        return _finish(nc, b, guards)
    b.alias("ysb", "xqsl", "qall", "OT", "B0")
    b.alias("s0", "Ssb0", "Ssb1")
    b.alias("s1", "Psb0", "Psb1")
    b.alias("s2", "kf")
    b.alias("xrfin", "kf")
    b.alias("hfs", "kf")
    b.alias("has", "kf")
    b.alias("ht1", "kf")
    b.alias("ht2", "kf")
    for j in range(3):
        b.alias(f"hs{j}", "vtok")
    b.alias("xrt", "wqb0")
    b.alias("xn2tok", "wqb1")
    b.alias("xn2f", "xnqf")
    b.alias("w13b0", "xntok")
    b.alias("w13b1", "xntok")
    b.alias("w2h0", "PT", "dtile", "Dmat", "Dmat2")
    b.alias("w2h1", "PT", "dtile", "Dmat", "Dmat2")

    SJ = (s0, s1, s2)

    def mlp_block(nf1, lhs1_dma, lhs3_dma, w2_dma, scale_fn):
        for f1c in range(nf1):
            p13 = f1c % 2
            b.emit("Q", lambda q, f1c=f1c, p13=p13: lhs1_dma(q, f1c, w13b[p13]),
                   writes=[f"w13b{p13}"], dma=True)
            b.emit("G", lambda g_, f1c=f1c, p13=p13: lhs3_dma(g_, f1c, w13b[2 + p13]),
                   writes=[f"w13b{2 + p13}"], dma=True)
            for wsel, pbank in ((p13, 0), (2 + p13, 1)):
                for j in range(3):
                    for ic in range(IC):
                        b.emit("T", lambda t, wsel=wsel, pbank=pbank, j=j, ic=ic:
                               t.matmul(ps[pbank][:], w13b[wsel][:, ic * 128:(ic + 1) * 128],
                                        SJ[j][:, ic * 512:ic * 512 + 512],
                                        start=(j == 0 and ic == 0),
                                        stop=(j == 2 and ic == IC - 1)),
                               reads=[f"w13b{wsel}", f"s{j}"], writes=[f"ps{pbank}"],
                               sig=(j == 2 and ic == IC - 1))
            b.emit("S", lambda s: s.activation(has[:], ps[0][:], ACT.Silu),
                   reads=["ps0"], writes=["has"])
            b.emit("V", lambda v: v.tensor_mul(hfs[:], has[:], ps[1][:]),
                   reads=["has", "ps1"], writes=["hfs"])
            b.emit("S", lambda s, f1c=f1c: s.copy(hs[0][:, f1c * 512:(f1c + 1) * 512], hfs[:]),
                   reads=["hfs"], writes=["hs0"])
            b.emit("V", lambda v, f1c=f1c:
                   v.tensor_sub(ht1[:], hfs[:], hs[0][:, f1c * 512:(f1c + 1) * 512]),
                   reads=["hfs", "hs0"], writes=["ht1"])
            b.emit("S", lambda s, f1c=f1c: s.copy(hs[1][:, f1c * 512:(f1c + 1) * 512], ht1[:]),
                   reads=["ht1"], writes=["hs1"])
            b.emit("V", lambda v, f1c=f1c:
                   v.tensor_sub(ht2[:], ht1[:], hs[1][:, f1c * 512:(f1c + 1) * 512]),
                   reads=["ht1", "hs1"], writes=["ht2"])
            b.emit("S", lambda s, f1c=f1c: s.copy(hs[2][:, f1c * 512:(f1c + 1) * 512], ht2[:]),
                   reads=["ht2"], writes=["hs2"])
        nh0 = min(6, nf1)
        for nsl in range(4):
            b.emit("Q", lambda q, nsl=nsl: w2_dma(q, nsl, w2h[0], 0, nh0),
                   writes=["w2h0"], dma=True)
            if nf1 > 6:
                b.emit("G", lambda g_, nsl=nsl: w2_dma(g_, nsl, w2h[1], 6, nf1 - 6),
                       writes=["w2h1"], dma=True)
            for Mt in range(4):
                pdn = 2 + ((nsl * 4 + Mt) % 2)
                pd = ps[pdn]
                for f1c in range(nf1):
                    wb_, wbn, fo = (w2h[0], "w2h0", f1c) if f1c < 6 else (w2h[1], "w2h1", f1c - 6)
                    for j in range(3):
                        b.emit("T", lambda t, f1c=f1c, j=j, Mt=Mt, pd=pd, wb_=wb_, fo=fo:
                               t.matmul(pd[:], hs[j][:, f1c * 512 + Mt * 128: f1c * 512 + Mt * 128 + 128],
                                        wb_[:, fo * 512:(fo + 1) * 512],
                                        start=(f1c == 0 and j == 0),
                                        stop=(f1c == nf1 - 1 and j == 2)),
                               reads=[f"hs{j}", wbn], writes=[f"ps{pdn}"],
                               sig=(f1c == nf1 - 1 and j == 2))
                scale_fn(Mt, nsl, pd, f"ps{pdn}")

    TRASH = 60000.0

    for grp in range(NGRP):
        b.alias("lnsq", "w13b2", "w13b3")
        b.alias("s0t", "hs0")
        b.alias("s1t", "hs0", "hs1")
        b.alias("s2t", "hs1", "hs2")
        for tt4 in range(GT):
            tt = grp * GT + tt4
            b.emit("G", lambda g, tt=tt: g.dma_start(xrt[:], xres_d.ap()[tt * 128:(tt + 1) * 128, :]),
                   reads=["xres_dram"], writes=["xrt"], dma=True)
            layernorm(xrt[:], "xrt", xn2tok[:], "xn2tok", lnsq, "lnsq")
            transpose16(xn2tok, "xn2tok", xn2f, "xn2f", lambda ic: ic * 128)
            for ic in range(IC):
                b.emit("T", lambda t, ic=ic, tt4=tt4:
                       t.matmul(ps[7][:, tt4 * 16:(tt4 + 1) * 16], xn2f[:, ic * 128:(ic + 1) * 128],
                                gw[:, ic * 16:(ic + 1) * 16],
                                start=(ic == 0), stop=(ic == IC - 1)),
                       reads=["xn2f", "c_gw"], writes=["ps7"], sig=(ic == IC - 1))
            b.emit("S", lambda s, tt4=tt4:
                   s.copy(ap3(s0, tt4 * 128, s0.shape[1], 128, 512, IC, 1, 128),
                          ap3(xn2f, 0, 2048, 128, 128, IC, 1, 128)),
                   reads=["xn2f"], writes=["s0"])
            b.emit("V", lambda v, tt4=tt4:
                   v.tensor_sub(ap3(lnsq, 0, 2048, 128, 128, IC, 1, 128),
                                ap3(xn2f, 0, 2048, 128, 128, IC, 1, 128),
                                ap3(s0, tt4 * 128, s0.shape[1], 128, 512, IC, 1, 128)),
                   reads=["xn2f", "s0"], writes=["lnsq"])
            b.emit("S", lambda s, tt4=tt4:
                   s.copy(ap3(s1, tt4 * 128, s1.shape[1], 128, 512, IC, 1, 128),
                          ap3(lnsq, 0, 2048, 128, 128, IC, 1, 128)),
                   reads=["lnsq"], writes=["s1"])
            b.emit("V", lambda v, tt4=tt4:
                   v.tensor_sub(ap3(lnsq, 0, 2048, 128, 128, IC, 1, 128),
                                ap3(lnsq, 0, 2048, 128, 128, IC, 1, 128),
                                ap3(s1, tt4 * 128, s1.shape[1], 128, 512, IC, 1, 128)),
                   reads=["lnsq", "s1"], writes=["lnsq"])
            b.emit("S", lambda s, tt4=tt4:
                   s.copy(ap3(s2, tt4 * 128, s2.shape[1], 128, 512, IC, 1, 128),
                          ap3(lnsq, 0, 2048, 128, 128, IC, 1, 128)),
                   reads=["lnsq"], writes=["s2"])
            b.emit("S", lambda s: s.copy(sjt[0][:], xn2tok[:]), reads=["xn2tok"], writes=["s0t"])
            b.emit("V", lambda v: v.tensor_sub(lnsq[:], xn2tok[:], sjt[0][:]),
                   reads=["xn2tok", "s0t"], writes=["lnsq"])
            b.emit("S", lambda s: s.copy(sjt[1][:], lnsq[:]), reads=["lnsq"], writes=["s1t"])
            b.emit("V", lambda v: v.tensor_sub(lnsq[:], lnsq[:], sjt[1][:]),
                   reads=["lnsq", "s1t"], writes=["lnsq"])
            b.emit("S", lambda s: s.copy(sjt[2][:], lnsq[:]), reads=["lnsq"], writes=["s2t"])
            for j in range(3):
                b.emit("G", lambda g, j=j, tt=tt:
                       g.dma_start(sj_d[j].ap()[tt * 128:(tt + 1) * 128, :], sjt[j][:]),
                       reads=[f"s{j}t"], writes=[f"sjd{j}"], dma=True)
            sc = scsb[:, 0:16]
            eq = scsb[:, 16:32]
            cnd = scsb[:, 32:48]
            ta = scsb[:, 48:64]
            sc3 = ppsb[:, 64:80]
            b.emit("S", lambda s, tt4=tt4: s.activation(sc, ps[7][:, tt4 * 16:(tt4 + 1) * 16], ACT.Sigmoid),
                   reads=["ps7"], writes=["sc"])
            b.emit("V", lambda v: v.reduce_max(st(SM1), sc, axis=AX), reads=["sc"], writes=["lnst"])
            b.emit("V", lambda v: v.tensor_scalar(eq, sc, st(SM1), None, op0=ALU.is_equal),
                   reads=["sc", "lnst"], writes=["eq"])
            b.emit("V", lambda v: v.tensor_mul(cnd, eq, iota16mb[:]), reads=["eq", "c_io16mb"], writes=["cnd"])
            b.emit("V", lambda v: v.tensor_scalar_add(cnd, cnd, MBIG), reads=["cnd"], writes=["cnd"])
            b.emit("V", lambda v: v.tensor_reduce(st(SI1), cnd, axis=AX, op=ALU.min),
                   reads=["cnd"], writes=["lnst"])
            b.emit("V", lambda v: v.tensor_scalar(eq, iota16[:], st(SI1), None, op0=ALU.is_equal),
                   reads=["c_io16", "lnst"], writes=["eq"])
            b.emit("V", lambda v, tt=tt: v.tensor_copy(m1sb[:, tt * 16:(tt + 1) * 16], eq),
                   reads=["eq"], writes=["c_m1"])
            b.emit("V", lambda v: v.tensor_scalar_mul(ta, eq, st(SM1)), reads=["eq", "lnst"], writes=["ta"])
            b.emit("V", lambda v: v.scalar_tensor_tensor(cnd, eq, -1e9, sc, op0=ALU.mult, op1=ALU.add),
                   reads=["eq", "sc"], writes=["cnd"])
            b.emit("V", lambda v: v.reduce_max(st(SM2), cnd, axis=AX), reads=["cnd"], writes=["lnst"])
            b.emit("V", lambda v: v.tensor_scalar(eq, cnd, st(SM2), None, op0=ALU.is_equal),
                   reads=["cnd", "lnst"], writes=["eq"])
            b.emit("V", lambda v: v.tensor_mul(sc3, eq, iota16mb[:]), reads=["eq", "c_io16mb"], writes=["sc3"])
            b.emit("V", lambda v: v.tensor_scalar_add(sc3, sc3, MBIG), reads=["sc3"], writes=["sc3"])
            b.emit("V", lambda v: v.tensor_reduce(st(SI1), sc3, axis=AX, op=ALU.min),
                   reads=["sc3"], writes=["lnst"])
            b.emit("V", lambda v: v.tensor_scalar(eq, iota16[:], st(SI1), None, op0=ALU.is_equal),
                   reads=["c_io16", "lnst"], writes=["eq"])
            b.emit("V", lambda v, tt=tt: v.tensor_copy(m2sb[:, tt * 16:(tt + 1) * 16], eq),
                   reads=["eq"], writes=["c_m2"])
            b.emit("V", lambda v: v.tensor_add(st(SSM), st(SM1), st(SM2)), reads=["lnst"], writes=["lnst"])
            b.emit("V", lambda v: v.reciprocal(st(SRS), st(SSM)), reads=["lnst"], writes=["lnst"])
            b.emit("V", lambda v: v.tensor_scalar_mul(eq, eq, st(SM2)), reads=["eq", "lnst"], writes=["eq"])
            b.emit("V", lambda v: v.tensor_add(ta, ta, eq), reads=["ta", "eq"], writes=["ta"])
            b.emit("V", lambda v, tt=tt: v.tensor_scalar_mul(dwsb[:, tt * 16:(tt + 1) * 16], ta, st(SRS)),
                   reads=["ta", "lnst"], writes=["dw"])
            b.emit("V", lambda v, tt=tt: v.scalar_tensor_tensor(cnd, m2sb[:, tt * 16:(tt + 1) * 16],
                                                                -1e9, cnd, op0=ALU.mult, op1=ALU.add),
                   reads=["c_m2", "cnd"], writes=["cnd"])
            b.emit("V", lambda v: v.reduce_max(st(SM2), cnd, axis=AX), reads=["cnd"], writes=["lnst"])
            b.emit("V", lambda v: v.tensor_scalar(eq, cnd, st(SM2), None, op0=ALU.is_equal),
                   reads=["cnd", "lnst"], writes=["eq"])
            b.emit("V", lambda v: v.tensor_mul(sc3, eq, iota16mb[:]), reads=["eq", "c_io16mb"], writes=["sc3"])
            b.emit("V", lambda v: v.tensor_scalar_add(sc3, sc3, MBIG), reads=["sc3"], writes=["sc3"])
            b.emit("V", lambda v: v.tensor_reduce(st(SI1), sc3, axis=AX, op=ALU.min),
                   reads=["sc3"], writes=["lnst"])
            b.emit("V", lambda v, tt=tt: v.tensor_scalar(m3sb[:, tt * 16:(tt + 1) * 16], iota16[:],
                                                         st(SI1), None, op0=ALU.is_equal),
                   reads=["c_io16", "lnst"], writes=["c_m3"])
        b.alias("w13b2", "lnsq")
        b.alias("w13b3", "lnsq")
        for j in range(3):
            b.alias(f"hs{j}", "s0t", "s1t", "s2t")
        for half in range(2):
            def sh_scale(Mt, nsl, pd, pdn, half=half):
                ysl = ysb[:, Mt * 2048 + nsl * 512: Mt * 2048 + (nsl + 1) * 512]
                if half == 0:
                    b.emit("S", lambda s, pd=pd, ysl=ysl: s.copy(ysl, pd[:]),
                           reads=[pdn], writes=["ysb"])
                else:
                    b.emit("V", lambda v, pd=pd, ysl=ysl: v.tensor_add(ysl, ysl, pd[:]),
                           reads=[pdn, "ysb"], writes=["ysb"])

            mlp_block(
                11,
                lambda q, f1c, dst, half=half: q.dma_start(
                    dst[:], bass.AP(sw1I_d, (half * 11 + f1c) * 128 * 2048, [[2048, 128], [1, 2048]])),
                lambda s, f1c, dst, half=half: s.dma_start(
                    dst[:], bass.AP(sw3I_d, (half * 11 + f1c) * 128 * 2048, [[2048, 128], [1, 2048]])),
                lambda q, nsl, dst, fbase, nchunk, half=half: q.dma_start(
                    dst[:, :nchunk * 512],
                    bass.AP(sw2I_d, (half * 4 + nsl) * 128 * (F1C * 512) + fbase * 512,
                            [[F1C * 512, 128], [1, nchunk * 512]])),
                sh_scale)
        for tt4 in range(GT):
            tt = grp * GT + tt4
            b.emit("G", lambda g, tt=tt, tt4=tt4:
                   g.dma_start(y_d.ap()[tt * 128:(tt + 1) * 128, :], ysb[:, tt4 * 2048:(tt4 + 1) * 2048]),
                   reads=["ysb"], writes=["y_dram"], dma=True)

    b.emit("G", lambda g: g.memset(zzsb[:], 0), writes=["c_zz"])
    b.emit("V", lambda v: v.memset(ppsb[:, 0:32], 0.0), writes=["pp0"])
    b.emit("G", lambda g: g.dma_start(ap3(gtab_d, 0, 32, 128, 1, 32, 1, 1), ap3(zzsb, 0, 32, 128, 1, 32, 1, 1)),
           reads=["c_zz"], writes=["gtabd"], dma=True)
    b.emit("G", lambda g: g.dma_start(ap3(dwtab_d, 0, 32, 128, 1, 32, 1, 1), ap3(ppsb, 0, 80, 128, 1, 32, 1, 1)),
           reads=["pp0"], writes=["dwtabd"], dma=True)
    b.emit("V", lambda v: v.memset(basesb[:, 0:32], 0.0), writes=["c_base"])
    for rnk in range(2):
        bcol = basesb[0:1, rnk * 16:rnk * 16 + 16]
        for tt in range(8):
            if rnk == 0:
                b.emit("V", lambda v, tt=tt: v.tensor_add(ppsb[:, 64:80], m1sb[:, tt * 16:(tt + 1) * 16],
                                                          m2sb[:, tt * 16:(tt + 1) * 16]),
                       reads=["c_m1", "c_m2"], writes=["sc3"])
            else:
                b.emit("V", lambda v, tt=tt: v.tensor_copy(ppsb[:, 64:80], m3sb[:, tt * 16:(tt + 1) * 16]),
                       reads=["c_m3"], writes=["sc3"])
            b.emit("T", lambda t: t.matmul(ps[6][:, 0:16], LT[:], ppsb[:, 64:80], start=True, stop=False),
                   reads=["c_lt", "sc3"], writes=["ps6"], sig=False)
            b.emit("T", lambda t, bcol=bcol: t.matmul(ps[6][:, 0:16], ones[0:1, :], bcol,
                                                      start=False, stop=True),
                   reads=["c_ones", "c_base"], writes=["ps6"])
            b.emit("T", lambda t: t.matmul(ps[5][0:1, 0:16], ones[:, 0:1], ppsb[:, 64:80],
                                           start=True, stop=True),
                   reads=["c_ones", "sc3"], writes=["ps5"])
            b.emit("V", lambda v: v.tensor_copy(ppsb[:, 0:16], ps[6][:, 0:16]), reads=["ps6"], writes=["pp0"])
            b.emit("V", lambda v, bcol=bcol: v.tensor_add(bcol, bcol, ps[5][0:1, 0:16]),
                   reads=["ps5", "c_base"], writes=["c_base"])
            b.emit("V", lambda v: v.tensor_scalar(ppsb[:, 16:32], ppsb[:, 0:16], 255.5, None, op0=ALU.is_ge),
                   reads=["pp0"], writes=["pp1"])
            b.emit("V", lambda v: v.tensor_add(ppsb[:, 32:48], ppsb[:, 0:16], ebase[:]),
                   reads=["pp0", "c_ebase"], writes=["pp2"])
            b.emit("V", lambda v: v.scalar_tensor_tensor(ppsb[:, 48:64], ppsb[:, 16:32], TRASH,
                                                         ppsb[:, 32:48], op0=ALU.mult, op1=ALU.add),
                   reads=["pp1", "pp2"], writes=["pp3"])
            b.emit("V", lambda v: v.tensor_scalar(ppsb[:, 64:80], ppsb[:, 0:16], 127.5, None, op0=ALU.is_ge),
                   reads=["pp0"], writes=["sc3"])
            b.emit("V", lambda v: v.scalar_tensor_tensor(ppsb[:, 32:48], ppsb[:, 0:16], 32.0,
                                                         ebase2[:], op0=ALU.mult, op1=ALU.add),
                   reads=["pp0", "c_ebase2"], writes=["pp2"])
            b.emit("V", lambda v: v.scalar_tensor_tensor(ppsb[:, 32:48], ppsb[:, 64:80], -4095.0,
                                                         ppsb[:, 32:48], op0=ALU.mult, op1=ALU.add),
                   reads=["sc3", "pp2"], writes=["pp2"])
            b.emit("V", lambda v: v.scalar_tensor_tensor(ppsb[:, 32:48], ppsb[:, 16:32], TRASH,
                                                         ppsb[:, 32:48], op0=ALU.mult, op1=ALU.add),
                   reads=["pp1", "pp2"], writes=["pp2"])
            ranks = ((0, m1sb, "c_m1"), (1, m2sb, "c_m2")) if rnk == 0 else ((2, m3sb, "c_m3"),)
            for k, msb, mnm in ranks:
                b.emit("V", lambda v, tt=tt, msb=msb: v.tensor_mul(ppsb[:, 0:16], msb[:, tt * 16:(tt + 1) * 16],
                                                                   ppsb[:, 48:64]),
                       reads=[mnm, "pp3"], writes=["pp0"])
                b.emit("V", lambda v, tt=tt, k=k: v.reduce_sum(slotf[:, tt * 3 + k: tt * 3 + k + 1],
                                                               ppsb[:, 0:16], axis=AX),
                       reads=["pp0"], writes=["c_slotf"])
                b.emit("V", lambda v, tt=tt, msb=msb: v.tensor_mul(ppsb[:, 0:16], msb[:, tt * 16:(tt + 1) * 16],
                                                                   ppsb[:, 32:48]),
                       reads=[mnm, "pp2"], writes=["pp0"])
                b.emit("V", lambda v, tt=tt, k=k: v.reduce_sum(tpsf[:, tt * 3 + k: tt * 3 + k + 1],
                                                               ppsb[:, 0:16], axis=AX),
                       reads=["pp0"], writes=["c_tpsf"])
                b.emit("V", lambda v, tt=tt, msb=msb: v.tensor_mul(ppsb[:, 0:16], msb[:, tt * 16:(tt + 1) * 16],
                                                                   dwsb[:, tt * 16:(tt + 1) * 16]),
                       reads=[mnm, "dw"], writes=["pp0"])
                b.emit("V", lambda v, tt=tt, k=k: v.reduce_sum(dwk[:, tt * 3 + k: tt * 3 + k + 1],
                                                               ppsb[:, 0:16], axis=AX),
                       reads=["pp0"], writes=["c_dwk"])
        if rnk == 0:
            b.emit("V", lambda v: v.tensor_copy(basesb[:, 16:32], basesb[:, 0:16]),
                   reads=["c_base"], writes=["c_base"])
    b.emit("V", lambda v: v.tensor_copy(sloti[:], slotf[:]), reads=["c_slotf"], writes=["c_sloti"])
    b.emit("V", lambda v: v.tensor_copy(tpsi[:], tpsf[:]), reads=["c_tpsf"], writes=["c_tpsi"])
    for tt in range(8):
        for k in range(3):
            b.emit("G", lambda g, tt=tt, k=k: g.indirect_dma_start(
                gtab_d.ap(), bass.IndirectOffsetOnAxis(tpsi[:, tt * 3 + k: tt * 3 + k + 1], 0),
                tokid[:, tt:tt + 1], None, bounds_check=b.greg, oob_is_err=False),
                   reads=["c_tpsi", "c_tokid"], writes=["gtabd"], dma=True)
            b.emit("G", lambda g, tt=tt, k=k: g.indirect_dma_start(
                dwtab_d.ap(), bass.IndirectOffsetOnAxis(tpsi[:, tt * 3 + k: tt * 3 + k + 1], 0),
                dwk[:, tt * 3 + k: tt * 3 + k + 1], None, bounds_check=b.greg, oob_is_err=False),
                   reads=["c_tpsi", "c_dwk"], writes=["dwtabd"], dma=True)
    b.emit("G", lambda g: g.dma_start(gtsb[:], gtab_d.ap().rearrange("(p c) one -> p (c one)", p=128)),
           reads=["gtabd"], writes=["c_gt"], dma=True)
    b.emit("G", lambda g: g.dma_start(dtsb[:], dwtab_d.ap().rearrange("(p c) one -> p (c one)", p=128)),
           reads=["dwtabd"], writes=["c_dt"], dma=True)

    b.alias("xgp0", "s0", "s1")
    b.alias("xgp1", "s0", "s1")
    b.alias("xgp2", "s0", "s1")
    b.alias("xgp3", "s0", "s1")
    b.alias("xgf0", "s2")
    b.alias("xgf1", "s2")
    b.alias("xgf2", "s2", "xrfin")
    for j in range(3):
        b.alias(f"hse{j}", "hs0", "hs1", "hs2")
    b.alias("arena_sb", "xrt", "xn2tok")
    b.alias("w2h2", "s1")
    b.alias("w2h3", "s1")
    def exp_gather(e):
        for hs_ in range(2):
            for j in range(3):
                par = (e * 6 + hs_ * 3 + j) % 4
                b.emit("G", lambda g, e=e, hs_=hs_, j=j, par=par: g.indirect_dma_start(
                    xgp[par][:], None, sj_d[j].ap(),
                    bass.IndirectOffsetOnAxis(gtsb[:, e * 2 + hs_: e * 2 + hs_ + 1], 0)),
                       reads=["c_gt", f"sjd{j}"], writes=[f"xgp{par}"], dma=True)
                for icg in range(4):
                    pb = ps[4 + (icg % 2)]
                    pn = f"ps{4 + (icg % 2)}"
                    for kk in range(4):
                        ic = icg * 4 + kk
                        b.emit("T", lambda t, ic=ic, kk=kk, pb=pb, par=par:
                               t.matmul(pb[:, kk * 128:(kk + 1) * 128],
                                        xgp[par][:, ic * 128:(ic + 1) * 128], identb[:],
                                        start=True, stop=True),
                               reads=[f"xgp{par}", "c_identb"], writes=[pn], sig=(kk == 3))
                    b.emit("V", lambda v, icg=icg, pb=pb, j=j, hs_=hs_:
                           v.tensor_copy(ap3(xgf[j], icg * 4 * 256 + hs_ * 128, 16 * 256, 128, 256, 4, 1, 128),
                                         ap3(pb, 0, 512, 128, 128, 4, 1, 128)),
                           reads=[pn], writes=[f"xgf{j}"])

    def exp_ab(e):
        for f1c in range(F1C):
            p13 = f1c % 2
            b.emit("Q", lambda q, f1c=f1c, p13=p13, e=e: q.dma_start(
                w13b[p13][:], bass.AP(w1I_d, (e * F1C + f1c) * 128 * 2048, [[2048, 128], [1, 2048]])),
                   writes=[f"w13b{p13}"], dma=True)
            b.emit("G", lambda g_, f1c=f1c, p13=p13, e=e: g_.dma_start(
                w13b[2 + p13][:], bass.AP(w3I_d, (e * F1C + f1c) * 128 * 2048, [[2048, 128], [1, 2048]])),
                   writes=[f"w13b{2 + p13}"], dma=True)
            for wsel, pbank in ((p13, 0), (2 + p13, 1)):
                for j in range(3):
                    for ic in range(IC):
                        b.emit("T", lambda t, wsel=wsel, pbank=pbank, j=j, ic=ic:
                               t.matmul(ps[pbank][:, 0:256], w13b[wsel][:, ic * 128:(ic + 1) * 128],
                                        xgf[j][:, ic * 256:(ic + 1) * 256],
                                        start=(j == 0 and ic == 0),
                                        stop=(j == 2 and ic == IC - 1)),
                               reads=[f"w13b{wsel}", f"xgf{j}"], writes=[f"ps{pbank}"],
                               sig=(j == 2 and ic == IC - 1))
            b.emit("S", lambda s: s.activation(has[:, 0:256], ps[0][:, 0:256], ACT.Silu),
                   reads=["ps0"], writes=["has"])
            b.emit("V", lambda v: v.tensor_mul(hfs[:, 0:256], has[:, 0:256], ps[1][:, 0:256]),
                   reads=["has", "ps1"], writes=["hfs"])
            b.emit("S", lambda s, f1c=f1c: s.copy(hse[0][:, f1c * 256:(f1c + 1) * 256], hfs[:, 0:256]),
                   reads=["hfs"], writes=["hse0"])
            b.emit("V", lambda v, f1c=f1c:
                   v.tensor_sub(ht1[:, 0:256], hfs[:, 0:256], hse[0][:, f1c * 256:(f1c + 1) * 256]),
                   reads=["hfs", "hse0"], writes=["ht1"])
            b.emit("S", lambda s, f1c=f1c: s.copy(hse[1][:, f1c * 256:(f1c + 1) * 256], ht1[:, 0:256]),
                   reads=["ht1"], writes=["hse1"])
            b.emit("V", lambda v, f1c=f1c:
                   v.tensor_sub(ht2[:, 0:256], ht1[:, 0:256], hse[1][:, f1c * 256:(f1c + 1) * 256]),
                   reads=["ht1", "hse1"], writes=["ht2"])
            b.emit("S", lambda s, f1c=f1c: s.copy(hse[2][:, f1c * 256:(f1c + 1) * 256], ht2[:, 0:256]),
                   reads=["ht2"], writes=["hse2"])

    def exp_down(e):
        for nsl in range(4):
            wset = (w2h[0], w2h[1], "w2h0", "w2h1") if (e * 4 + nsl) % 2 == 0 else \
                   (w2hB[0], w2hB[1], "w2h2", "w2h3")
            b.emit("Q", lambda q, nsl=nsl, e=e, wset=wset: q.dma_start(
                wset[0][:], bass.AP(w2I_d, (e * 4 + nsl) * 128 * (F1C * 512),
                                    [[F1C * 512, 128], [1, 6 * 512]])),
                   writes=[wset[2]], dma=True)
            b.emit("G", lambda g_, nsl=nsl, e=e, wset=wset: g_.dma_start(
                wset[1][:, :5 * 512], bass.AP(w2I_d, (e * 4 + nsl) * 128 * (F1C * 512) + 6 * 512,
                                              [[F1C * 512, 128], [1, 5 * 512]])),
                   writes=[wset[3]], dma=True)
            for Mt in range(2):
                pdn = 2 + ((nsl * 2 + Mt) % 2)
                pd = ps[pdn]
                for f1c in range(F1C):
                    wb_, wbn, fo = (wset[0], wset[2], f1c) if f1c < 6 else (wset[1], wset[3], f1c - 6)
                    for j in range(3):
                        b.emit("T", lambda t, f1c=f1c, j=j, Mt=Mt, pd=pd, wb_=wb_, fo=fo:
                               t.matmul(pd[:], hse[j][:, f1c * 256 + Mt * 128: f1c * 256 + Mt * 128 + 128],
                                        wb_[:, fo * 512:(fo + 1) * 512],
                                        start=(f1c == 0 and j == 0),
                                        stop=(f1c == F1C - 1 and j == 2)),
                               reads=[f"hse{j}", wbn], writes=[f"ps{pdn}"],
                               sig=(f1c == F1C - 1 and j == 2))
                b.emit("V", lambda v, Mt=Mt, nsl=nsl, pd=pd, e=e:
                       v.tensor_scalar_mul(arena_sb[:, Mt * 2048 + nsl * 512: Mt * 2048 + (nsl + 1) * 512],
                                           pd[:], dtsb[:, e * 2 + Mt: e * 2 + Mt + 1]),
                       reads=[f"ps{pdn}", "c_dt"], writes=["arena_sb"])
        for Mt in range(2):
            b.emit("G", lambda g, e=e, Mt=Mt: g.dma_start(
                arena_d.ap()[e * 256 + Mt * 128: e * 256 + (Mt + 1) * 128, :],
                arena_sb[:, Mt * 2048:(Mt + 1) * 2048]),
                   reads=["arena_sb"], writes=["arenad"], dma=True)

    exp_gather(0)
    for e in range(E):
        exp_ab(e)
        if e + 1 < E:
            exp_gather(e + 1)
        exp_down(e)

    b.alias("yg", "xgf0")
    b.alias("ygk", "xgf1")
    b.alias("xrfin", "xgf2")
    for tt in range(8):
        b.emit("G", lambda g, tt=tt: g.dma_start(yg[:], y_d.ap()[tt * 128:(tt + 1) * 128, :]),
               reads=["y_dram"], writes=["yg"], dma=True)
        for k in range(3):
            b.emit("G", lambda g, tt=tt, k=k: g.indirect_dma_start(
                ygk[:], None, arena_d.ap(),
                bass.IndirectOffsetOnAxis(sloti[:, tt * 3 + k: tt * 3 + k + 1], 0),
                bounds_check=b.greg, oob_is_err=False),
                   reads=["c_sloti", "arenad"], writes=["ygk"], dma=True)
            b.emit("V", lambda v: v.tensor_add(yg[:], yg[:], ygk[:]),
                   reads=["ygk", "yg"], writes=["yg"])
        b.emit("G", lambda g, tt=tt: g.dma_start(xrfin[:], xres_d.ap()[tt * 128:(tt + 1) * 128, :]),
               reads=["xres_dram"], writes=["xrfin"], dma=True)
        b.emit("V", lambda v: v.tensor_add(yg[:], yg[:], xrfin[:]),
               reads=["xrfin", "yg"], writes=["yg"])
        b.emit("G", lambda g, tt=tt: g.dma_start(out_d.ap()[tt * 128:(tt + 1) * 128, :], yg[:]),
               reads=["yg"], writes=["outd"], dma=True)

    return _finish(nc, b, guards)


def _finish(nc, b, guards):
    finals = [(b.sems[k], v) for k, v in sorted(b.dcnt.items())
              if k.startswith("d_") and (k.startswith("d_out") or k.startswith("d_dbg") or k == "d_xres_dram")]
    with nc.Block() as block:
        @block.tensor
        def _(eng):
            for f in b.ops["T"]:
                f(eng)

        @block.scalar
        def _(eng):
            for f in b.ops["S"]:
                f(eng)

        @block.vector
        def _(eng):
            for f in b.ops["V"]:
                f(eng)

        @block.sync
        def _(eng):
            for f in b.ops["Q"]:
                f(eng)

        @block.gpsimd
        def _(eng):
            b.greg = eng.alloc_register("bnd")
            eng.reg_mov(b.greg, 4095)
            for f in b.ops["G"]:
                f(eng)
            for s_, v_ in finals:
                eng.wait_ge(s_, v_)

    for g in reversed(guards):
        g.__exit__(None, None, None)
    return nc


def host_inputs(L0_attn_x, L0_attn_Wq, L0_attn_Wk, L0_attn_Wv, L0_attn_Wo,
                L0_ffn_gate_w, L0_ffn_w1, L0_ffn_w2, L0_ffn_w3,
                L0_ffn_sw1, L0_ffn_sw2, L0_ffn_sw3):
    BF = ml_dtypes.bfloat16
    x = np.ascontiguousarray(np.asarray(L0_attn_x, dtype=np.float32))
    shared = dict(
        wqI=np.ascontiguousarray(np.asarray(L0_attn_Wq, np.float32).T
                                 .reshape(16, 128, 16, 128).transpose(2, 1, 0, 3)
                                 .reshape(16, 128, 2048)),
        wkT=np.ascontiguousarray(np.asarray(L0_attn_Wk, np.float32).T),
        wvT=np.ascontiguousarray(np.asarray(L0_attn_Wv, np.float32).T),
        woT=np.ascontiguousarray(np.asarray(L0_attn_Wo, np.float32).T),
        gwT=np.ascontiguousarray(np.asarray(L0_ffn_gate_w, np.float32).T),
        w1I=np.ascontiguousarray(np.asarray(L0_ffn_w1, BF).transpose(0, 2, 1)
                                 .reshape(16, 16, 128, 11, 128).transpose(0, 3, 2, 1, 4)
                                 .reshape(16, 11, 128, 2048)),
        w3I=np.ascontiguousarray(np.asarray(L0_ffn_w3, BF).transpose(0, 2, 1)
                                 .reshape(16, 16, 128, 11, 128).transpose(0, 3, 2, 1, 4)
                                 .reshape(16, 11, 128, 2048)),
        w2I=np.ascontiguousarray(np.asarray(L0_ffn_w2, BF).transpose(0, 2, 1)
                                 .reshape(16, 11, 128, 4, 512).transpose(0, 3, 2, 1, 4)
                                 .reshape(16, 4, 128, 11 * 512)),
        sw1I=np.ascontiguousarray(np.asarray(L0_ffn_sw1, BF).T
                                  .reshape(16, 128, 22, 128).transpose(2, 1, 0, 3)
                                  .reshape(22, 128, 2048)),
        sw3I=np.ascontiguousarray(np.asarray(L0_ffn_sw3, BF).T
                                  .reshape(16, 128, 22, 128).transpose(2, 1, 0, 3)
                                  .reshape(22, 128, 2048)),
        sw2I=np.ascontiguousarray(np.asarray(L0_ffn_sw2, BF).T
                                  .reshape(2, 11, 128, 4, 512).transpose(0, 3, 2, 1, 4)
                                  .reshape(2, 4, 128, 11 * 512)),
        ionegf=np.ascontiguousarray(np.tile(-np.arange(512, dtype=np.float32)[None, :], (128, 1))),
        slopearg=np.ascontiguousarray(np.tile((2.0 ** np.arange(1, 17, dtype=np.float32))[None, :], (128, 1))),
        iota16=np.ascontiguousarray(np.tile(np.arange(16, dtype=np.float32)[None, :], (128, 1))),
        iota16mb=np.ascontiguousarray(np.tile(np.arange(16, dtype=np.float32)[None, :] - np.float32(MBIG), (128, 1))),
        cst=np.ascontiguousarray(np.tile(np.array([1e-5, 128.0, 0, 0, 0, 0, 0, 0], np.float32)[None, :], (128, 1))),
        ident=np.eye(128, dtype=np.float32),
        identb=np.eye(128, dtype=BF),
        ltexc=np.ascontiguousarray(np.triu(np.ones((128, 128), np.float32), 1)),
        ones128=np.ones((128, 128), np.float32),
        ebase16=np.ascontiguousarray(np.tile((np.arange(16, dtype=np.float32) * 256.0)[None, :], (128, 1))),
        ebase2=np.ascontiguousarray(np.tile((np.arange(16, dtype=np.float32) * 2.0)[None, :], (128, 1))),
        tokid=np.ascontiguousarray((np.arange(128, dtype=np.int32)[:, None]
                                    + 128 * np.arange(8, dtype=np.int32)[None, :])),
    )
    in_maps, slot_tiles = [], []
    for c in range(8):
        bb, hh = c // 2, c % 2
        gts = [(14 if hh == 0 else 15) - 2 * s for s in range(8)]
        slot_tiles.append(gts)
        xqr = np.concatenate([x[bb, g * 128:(g + 1) * 128, :] for g in gts], axis=0)
        pv = np.zeros((128, 32), np.float32)
        p = np.arange(128, dtype=np.float32)
        for s_ in range(8):
            for kb in range(4):
                pv[:, s_ * 4 + kb] = p + gts[s_] * 128 - 512 * kb
        m = dict(shared)
        m["xq"] = np.ascontiguousarray(xqr)
        m["xkv"] = np.ascontiguousarray(x[bb])
        m["pvec"] = pv
        in_maps.append(m)
    return in_maps, slot_tiles


def kernel(**inputs):
    in_maps, slot_tiles = host_inputs(**inputs)
    nc = build_nc()
    res = run_bass_kernel_spmd(nc, in_maps, core_ids=list(range(8)))
    out = np.empty((B, S, D), dtype=np.float32)
    for c in range(8):
        o = np.asarray(res.results[c]["out"])
        bb = c // 2
        for s_, g in enumerate(slot_tiles[c]):
            out[bb, g * 128:(g + 1) * 128, :] = o[s_ * 128:(s_ + 1) * 128, :]
    return out
